# version 12
# baseline (speedup 1.0000x reference)
import math
import numpy as np
import ml_dtypes
import concourse.bass as bass
import concourse.mybir as mybir
from concourse.bass_utils import run_bass_kernel_spmd

F32 = mybir.dt.float32
BF16 = mybir.dt.bfloat16
AF = mybir.ActivationFunctionType
AO = mybir.AluOpType
AX = mybir.AxisListType
BFD = ml_dtypes.bfloat16

NCORES = 8
S, D, T = 8192, 2048, 1024
NH, DH, KVH = 16, 128, 4
FH = 8192
DCH = D // 128
FCH = FH // 128
TBF = 256
NTBF = T // TBF
QB = 256
SCALE = float(1.0 / np.sqrt(128.0))
INV_SQRT_FH = float(1.0 / np.sqrt(float(FH)))
EPS = 1e-5
BIGM = 60000.0


def blocks_of(c):
    return [16 * i + 2 * c + j for i in range(4) for j in range(2)]


def tok_idx(c):
    return np.concatenate([np.arange(g * 128, (g + 1) * 128) for g in blocks_of(c)])


PERM = np.empty(FH, np.int64)
for _m in range(64):
    _q, _r = _m // 4, _m % 4
    for _p in range(128):
        PERM[_m * 128 + _p] = _q * 512 + _p * 4 + _r


def alibi_slopes16():
    start = 2 ** (-(2 ** (-(math.log2(16) - 3))))
    return [float(start * start ** i) for i in range(16)]


SLOPES = alibi_slopes16()
KB_GBLK = [blocks_of(c)[b] for c in range(NCORES) for b in range(8)]


def hadamard(n):
    H = np.array([[1.0]], np.float32)
    while H.shape[0] < n:
        H = np.block([[H, H], [H, -H]]).astype(np.float32)
    return H


ENGS = ("sync", "gpsimd", "tensor", "vector", "scalar")


class Sem:
    __slots__ = ("h", "n", "name")

    def __init__(self, h, name):
        self.h = h
        self.n = 0
        self.name = name


class Prog:
    def __init__(self, nc, sems):
        self.nc = nc
        self.ops = {e: [] for e in ENGS}
        self.sems = sems
        self.lastw = {}
        self.esem = {"tensor": sems["T"], "vector": sems["V"],
                     "scalar": sems["S"], "gpsimd": sems["G"], "sync": sems["Y"]}
        self.dq = {}
        for q, pre in (("sync", "DY"), ("gpsimd", "DG"), ("scalar", "DS")):
            ring = [sems[pre + str(i)] for i in range(4)]
            self.dq[q] = {"ring": ring, "idx": 0, "cur": ring[0],
                          "prev": [None, None, None, None]}
        self.cc_last = None

    def emit(self, eng, fn, waits=(), inc=None):
        if eng in ("vector", "scalar"):
            s = self.esem[eng]
            if s.n > 0:
                waits = list(waits) + [(s, s.n)]
        if inc is True:
            s = self.esem[eng]
            s.n += 1
            incr = (s, s.n, 1)
        elif inc is not None:
            s, k = inc
            s.n += k
            incr = (s, s.n, k)
        else:
            incr = None
        ws = []
        for wv in waits:
            if wv is None:
                continue
            s, v = wv
            key = (eng, s.name)
            if self.lastw.get(key, -1) >= v:
                continue
            self.lastw[key] = v
            ws.append((s.h, v))
        inc_h = None if incr is None else (incr[0].h, incr[2])
        tok = None if incr is None else (incr[0], incr[1])

        def op(eng_obj, _fn=fn, _ws=tuple(ws), _inc=inc_h):
            for (h, v) in _ws:
                eng_obj.wait_ge(h, v)
            ins = _fn(eng_obj)
            if _inc is not None:
                ins.then_inc(_inc[0], _inc[1])
        self.ops[eng].append(op)
        return tok

    def dma(self, q, out, in_, waits=(), slow=False, new_group=True):
        st = self.dq[q]
        waits = [w for w in waits if w is not None]
        if new_group:
            st["idx"] = (st["idx"] + 1) % 4
            st["cur"] = st["ring"][st["idx"]]
            prev = st["prev"][st["idx"]]
            if prev is not None:
                waits = waits + [prev]
        sem = st["cur"]
        if slow:
            tok = self.emit(q, lambda e, o=out, i=in_:
                            e.dma_start(out=o, in_=i,
                                        allow_slow_non_contiguous=True),
                            waits=waits, inc=(sem, 16))
        else:
            tok = self.emit(q, lambda e, o=out, i=in_: e.dma_start(out=o, in_=i),
                            waits=waits, inc=(sem, 16))
        st["prev"][st["idx"]] = tok
        return tok

    def wait_only(self, eng, waits):
        ws = []
        for wv in waits:
            if wv is None:
                continue
            s, v = wv
            key = (eng, s.name)
            if self.lastw.get(key, -1) >= v:
                continue
            self.lastw[key] = v
            ws.append((s.h, v))
        if not ws:
            return

        def op(eng_obj, _ws=tuple(ws)):
            for (h, v) in _ws:
                eng_obj.wait_ge(h, v)
        self.ops[eng].append(op)

    def barrier(self):
        toks = [(s, s.n) for s in self.sems.values() if s.n > 0]
        for e in ENGS:
            self.wait_only(e, toks)


def build_nc():
    nc = bass.Bass("TRN2", target_bir_lowering=False)

    di = {}
    di["x"] = nc.dram_tensor("x", [T, D], F32, kind="ExternalInput")
    di["qpos"] = nc.dram_tensor("qpos", [T], F32, kind="ExternalInput")
    TSH = {"wq": (D, D), "wk": (D, D), "wv": (D, D), "wo": (D, D),
           "w1": (FH, D), "w3": (FH, D), "w2": (D, FH),
           "l1q": (D, D), "l1k": (512, D), "l1v": (512, D), "l1o": (D, D)}
    for name, (od, idim) in TSH.items():
        di[name + "_sh"] = nc.dram_tensor(name + "_sh", [od, idim // 8], F32,
                                          kind="ExternalInput")
    di["wg_sh"] = nc.dram_tensor("wg_sh", [D // 8, FH], F32, kind="ExternalInput")
    di["wu_sh"] = nc.dram_tensor("wu_sh", [D // 8, FH], F32, kind="ExternalInput")
    di["wd_sh"] = nc.dram_tensor("wd_sh", [FH // 8, D], F32, kind="ExternalInput")
    for bn in ("bq", "bk", "bv", "bo"):
        di[bn] = nc.dram_tensor(bn, [D], F32, kind="ExternalInput")
    y = nc.dram_tensor("y", [T, D], F32, kind="ExternalOutput")

    dint = {}

    def idram(name, shape, dt=BF16):
        dint[name] = nc.dram_tensor(name, shape, dt)
        return dint[name]

    for name, (od, idim) in TSH.items():
        idram(name + "_tsh", [idim // 8, od])
        if name not in ("w2",):
            idram(name + "_t", [idim, od])
    idram("w2_psh", [DCH, FH // 8, 128])
    idram("w2_pm", [DCH, FH, 128])
    idram("wd_psh", [DCH, FH // 8, 128])
    idram("wd_pm", [DCH, FH, 128])
    idram("wg_bsh", [D // 8, FH])
    idram("wg_b", [D, FH])
    idram("wu_bsh", [D // 8, FH])
    idram("wu_b", [D, FH])
    idram("wd_bsh", [FH // 8, D])
    idram("k0_sh", [D, T])
    idram("k0_ag", [NCORES * D, T])
    idram("v0_sh", [NH * T, DH])
    idram("v0_ag", [NCORES * NH * T, DH])
    idram("k1_sh", [512, T])
    idram("k1_ag", [NCORES * 512, T])
    idram("v1_sh", [KVH * T, DH])
    idram("v1_ag", [NCORES * KVH * T, DH])
    idram("nrm0_in", [1, 16], F32)
    idram("nrm0_out", [1, 16], F32)
    idram("nrm1_in", [1, 16], F32)
    idram("nrm1_out", [1, 16], F32)

    id_bf = nc.inline_tensor(np.eye(128, dtype=BFD), name="id_bf")
    id_f32c = nc.inline_tensor(np.eye(128, dtype=np.float32), name="id_f32c")
    h128_bf = nc.inline_tensor(hadamard(128).astype(BFD), name="h128_bf")
    ones_col_bfc = nc.inline_tensor(np.ones((128, 1), dtype=BFD), name="ones_col_bfc")
    ones_col_fc = nc.inline_tensor(np.ones((128, 1), dtype=np.float32), name="ones_col_fc")
    ones_row_fc = nc.inline_tensor(np.ones((1, 128), dtype=np.float32), name="ones_row_fc")
    negones_row_fc = nc.inline_tensor(np.full((1, 128), -1.0, dtype=np.float32),
                                      name="negones_row_fc")
    kpos_np = np.zeros((128, 64), np.float32)
    for kb in range(64):
        kpos_np[:, kb] = KB_GBLK[kb] * 128 + np.arange(128)
    kpos_c = nc.inline_tensor(kpos_np, name="kpos_c")
    cvals = sorted({-1.0, 0.0, 1.0, 2.0, -2.0, 0.5, 1.05, EPS, 1.0 / 2048.0, SCALE,
                    INV_SQRT_FH, BIGM, -BIGM} | {float(s) for s in SLOPES}
                   | {float(-s) for s in SLOPES})
    cv_fc = nc.inline_tensor(np.tile(np.asarray(cvals, np.float32)[None, :], (128, 1)),
                             name="cv_fc")
    cv_bc = nc.inline_tensor(np.tile(np.asarray(cvals, BFD)[None, :], (128, 1)),
                             name="cv_bc")

    ctxs = []
    sems = {}
    for name in ("T", "V", "S", "G", "Y", "CC0", "CC1", "CC2", "CC3",
                 "DY0", "DY1", "DY2", "DY3", "DG0", "DG1", "DG2", "DG3",
                 "DS0", "DS1", "DS2", "DS3"):
        g = nc.semaphore(name)
        sems[name] = Sem(g.__enter__(), name)
        ctxs.append(g)

    sbufs = {}

    def sbuf(name, shape, dt):
        g = nc.sbuf_tensor(name, shape, dt)
        t = g.__enter__()
        ctxs.append(g)
        sbufs[name] = t
        return t

    X = sbuf("xres", [128, DCH * T], F32)
    A = sbuf("arA", [128, DCH * T], BF16)
    B = sbuf("arB", [128, 2 * DCH * T], BF16)
    W = sbuf("arW", [128, 8192], BF16)
    SCR = sbuf("scr", [128, 4096], F32)
    MISC = sbuf("misc", [128, 640], F32)
    CONS = sbuf("cons", [128, 600], F32)
    CONB = sbuf("conb", [128, 512], BF16)
    ROWS = sbuf("rows", [1, 2048], F32)
    QH = sbuf("qh", [128, 1024], BF16)

    g = nc.psum_tensor("psA", [128, 1024], F32)
    psA = g.__enter__()
    ctxs.append(g)
    g = nc.psum_tensor("psB", [128, 1024], F32)
    psB = g.__enter__()
    ctxs.append(g)
    g = nc.psum_tensor("psC", [128, 512], F32)
    psC = g.__enter__()
    ctxs.append(g)
    g = nc.psum_tensor("psD", [128, 512], F32)
    psD = g.__enter__()
    ctxs.append(g)
    g = nc.psum_tensor("psT", [128, 1024], BF16)
    psT = g.__enter__()
    ctxs.append(g)
    g = nc.psum_tensor("ps7", [128, 512], F32)
    ps7 = g.__enter__()
    ctxs.append(g)

    class PsView:
        def __init__(self, h, off):
            self.h = h
            self.off = off

        def __getitem__(self, idx):
            pi, fi = idx
            return self.h[pi, self.off + (fi.start or 0):
                          self.off + (fi.stop if fi.stop is not None else 512)]

    ps = [PsView(psA, 0), PsView(psA, 512), PsView(psB, 0), PsView(psB, 512),
          PsView(psC, 0), PsView(psD, 0)]

    p = Prog(nc, sems)
    BP = 2 * DCH * T
    AP_ = bass.AP

    C_ID, C_ONE, C_KPOS, C_CV, C_BIAS, C_OROW, C_NROW = 0, 128, 129, 200, 260, 330, 458
    B_ID, B_H128, B_ONE, B_CV = 0, 128, 256, 260
    for i, v in enumerate(cvals):
        nc.const_aps.register(F32, v, CONS[:, C_CV + i:C_CV + i + 1])
        nc.const_aps.register(BF16, v, CONB[:, B_CV + i:B_CV + i + 1])
    R_MEAN, R_INV = 0, 1024
    R_KNL, R_KMG, R_QM, R_PRD, R_SQ, R_CP, R_CN = 3100, 3116, 3132, 3148, 3164, 3180, 3196
    R_RCP = 3212
    R_PACK = 3500
    ONES_ROW = CONS[0:1, C_OROW:C_OROW + 128]
    NEG_ROW = CONS[0:1, C_NROW:C_NROW + 128]
    ONE_COL_F = CONS[:, C_ONE:C_ONE + 1]
    ONE_COL_B = CONB[:, B_ONE:B_ONE + 1]

    p.dma("sync", CONS[:, C_ID:C_ID + 128], id_f32c[:, :])
    p.dma("sync", CONS[:, C_ONE:C_ONE + 1], ones_col_fc[:, :], new_group=False)
    p.dma("sync", CONS[:, C_KPOS:C_KPOS + 64], kpos_c[:, :], new_group=False)
    p.dma("sync", CONS[:, C_CV:C_CV + len(cvals)], cv_fc[:, :], new_group=False)
    p.dma("sync", CONS[0:1, C_OROW:C_OROW + 128], ones_row_fc[0:1, :], new_group=False)
    p.dma("sync", CONS[0:1, C_NROW:C_NROW + 128], negones_row_fc[0:1, :], new_group=False)
    p.dma("sync", CONB[:, B_ID:B_ID + 128], id_bf[:, :], new_group=False)
    p.dma("sync", CONB[:, B_H128:B_H128 + 128], h128_bf[:, :], new_group=False)
    p.dma("sync", CONB[:, B_ONE:B_ONE + 1], ones_col_bfc[:, :], new_group=False)
    t_consts = p.dma("sync", CONB[:, B_CV:B_CV + len(cvals)], cv_bc[:, :],
                     new_group=False)
    for i, bn in enumerate(("bq", "bk", "bv", "bo")):
        t_consts = p.dma("sync", CONS[:, C_BIAS + 16 * i:C_BIAS + 16 * (i + 1)],
                         AP_(di[bn], 0, [[1, 128], [128, 16]]), slow=True,
                         new_group=False)

    ptok = {}

    def psw(bank):
        t = ptok.get(bank)
        return [t] if t is not None else []

    def mm(bank_ap, lhsT, rhs, start, stop, waits=(), inc=None):
        return p.emit("tensor",
                      lambda e, o=bank_ap, l=lhsT, r=rhs, st=start, sp=stop:
                      e.matmul(o, l, r, start=st, stop=sp),
                      waits=waits, inc=inc)

    def ev(fn, waits=(), inc=True, eng="vector"):
        return p.emit(eng, fn, waits=waits, inc=inc)

    scr_cons = [None, None]
    for tch in range(8):
        half = tch % 2
        base = half * 2048
        tdma = p.dma("sync", SCR[:, base:base + 2048],
                     di["x"][tch * 128:(tch + 1) * 128, :],
                     waits=[scr_cons[half], t_consts])
        last_cp = None
        for grp in range(4):
            bank = grp % 2
            tmm = None
            for j in range(4):
                dcb = grp * 4 + j
                tmm = p.emit(
                    "tensor",
                    lambda e, o=ps[bank][:, j * 128:(j + 1) * 128],
                    i_=SCR[:, base + dcb * 128:base + (dcb + 1) * 128],
                    idn=CONS[:, C_ID:C_ID + 128]: e.transpose(o, i_, idn),
                    waits=([tdma] + psw(bank)) if j == 0 else [],
                    inc=True if j == 3 else None)
            tcp = ev(lambda e, o=AP_(X.ap().tensor, (grp * 4) * T + tch * 128,
                                     [[DCH * T, 128], [T, 4], [1, 128]]),
                     i_=ps[bank][:, 0:512], _w=None:
                     e.tensor_copy(o, i_), waits=[tmm])
            ptok[bank] = tcp
            last_cp = tcp
        scr_cons[half] = last_cp

    prep_acons = [None]

    def prep_transpose(name, od, isz):
        ish = di[name + "_sh"]
        osh = dint[name + "_tsh"]
        nob = od // 128
        toks = []
        per = max(1, 2048 // isz * 4)
        ofs = 0
        while ofs < nob:
            cnt = min(per, nob - ofs)
            tk = p.dma("gpsimd", B[:, ofs * isz:(ofs + cnt) * isz],
                       AP_(ish, ofs * 128 * isz,
                           [[isz, 128], [128 * isz, cnt], [1, isz]]),
                       new_group=(ofs == 0))
            toks.append(tk)
            ofs += cnt
        tld = toks[-1]
        tst = None
        a_cons = prep_acons[0]
        for icb in range(isz // 128):
            for ob4 in range((nob + 3) // 4):
                jn = min(4, nob - ob4 * 4)
                tmm = None
                for j in range(jn):
                    ocb = ob4 * 4 + j
                    tmm = p.emit(
                        "tensor",
                        lambda e, o=psT[:, j * 128:(j + 1) * 128],
                        i_=B[:, ocb * isz + icb * 128:ocb * isz + icb * 128 + 128],
                        idn=CONB[:, B_ID:B_ID + 128]: e.transpose(o, i_, idn),
                        waits=([tld, t_consts, a_cons] + psw("T")) if j == 0 else [],
                        inc=True if j == jn - 1 else None)
                tcp = ev(lambda e, o=AP_(B.ap().tensor, 16384 + ob4 * 4 * 128,
                                         [[BP, 128], [1, jn * 128]]),
                         i_=psT[:, 0:jn * 128]: e.tensor_copy(o, i_),
                         waits=[tmm])
                ptok["T"] = tcp
            tst = p.dma("gpsimd", osh[icb * 128:(icb + 1) * 128, :],
                        B[:, 16384:16384 + od], waits=[ptok["T"]])
            a_cons = tst
        prep_acons[0] = tst
        return tst

    cc_ring = {"idx": 0, "prev": [None, None, None, None]}

    def do_cc(kind, op, in_ap, out_ap, waits):
        cc_ring["idx"] = (cc_ring["idx"] + 1) % 4
        idx = cc_ring["idx"]
        sem = sems["CC" + str(idx)]
        w = list(waits) + ([cc_ring["prev"][idx]] if cc_ring["prev"][idx] else [])
        tok = p.emit("gpsimd",
                     lambda e, i=in_ap, o=out_ap, k=kind, o_=op:
                     e.collective_compute(k, o_,
                                          replica_groups=[list(range(NCORES))],
                                          ins=[i], outs=[o]),
                     waits=w, inc=(sem, 1))
        cc_ring["prev"][idx] = tok
        return tok

    def do_ag(in_ap, out_ap, waits):
        return do_cc("AllGather", AO.bypass, in_ap, out_ap, waits)

    ag = {}

    prep_st = {}

    def prep_one(name, ag_now=True):
        od, idim = TSH[name]
        tst = prep_transpose(name, od, idim // 8)
        prep_st[name] = tst
        if not ag_now:
            return
        prep_ag(name)

    def prep_ag(name):
        od, idim = TSH[name]
        tst = prep_st[name]
        if name == "w2":
            lastc = None
            for oc in range(DCH):
                lastc = p.dma("gpsimd",
                              AP_(dint["w2_psh"], oc * (FH // 8) * 128,
                                  [[128, FH // 8], [1, 128]]),
                              AP_(dint["w2_tsh"], oc * 128,
                                  [[D, FH // 8], [1, 128]]),
                              waits=[tst] if oc == 0 else [],
                              new_group=(oc == 0))
            agl = None
            for oc in range(DCH):
                agl = do_ag(AP_(dint["w2_psh"], oc * (FH // 8) * 128,
                                [[128, FH // 8], [1, 128]]).opt(),
                            AP_(dint["w2_pm"], oc * FH * 128,
                                [[128, FH], [1, 128]]).opt(),
                            [lastc])
            ag["w2"] = agl
        else:
            ag[name] = do_ag(dint[name + "_tsh"].ap().opt(),
                             dint[name + "_t"].ap().opt(), [tst])

    def prep_cast(src, dst, rows, cols):
        sts = []
        for i, r0 in enumerate(range(0, rows, 128)):
            half = i % 2
            base = half * cols
            if cols > 8192:
                raise ValueError("prep_cast cols too big")
            tk = p.dma("gpsimd", B[:, base:base + cols], src[r0:r0 + 128, :],
                       waits=[sts[-2]] if len(sts) >= 2 else [])
            tst = p.dma("gpsimd", dst[r0:r0 + 128, :], B[:, base:base + cols],
                        waits=[tk])
            sts.append(tst)
        return sts

    def prep_rest():
        for name in ("w1", "w3", "w2", "wq", "wo", "l1q", "l1k", "l1v", "l1o"):
            prep_one(name, ag_now=False)
        for name in ("w1", "w3", "w2", "wq", "wo", "l1q", "l1k", "l1v", "l1o"):
            prep_ag(name)
        tg = prep_cast(di["wg_sh"], dint["wg_bsh"], D // 8, FH)
        ag["wg"] = do_ag(dint["wg_bsh"].ap().opt(), dint["wg_b"].ap().opt(), tg)
        tu = prep_cast(di["wu_sh"], dint["wu_bsh"], D // 8, FH)
        ag["wu"] = do_ag(dint["wu_bsh"].ap().opt(), dint["wu_b"].ap().opt(), tu)
        td = prep_cast(di["wd_sh"], dint["wd_bsh"], FH // 8, D)
        lastc = None
        for oc in range(DCH):
            lastc = p.dma("gpsimd",
                          AP_(dint["wd_psh"], oc * (FH // 8) * 128,
                              [[128, FH // 8], [1, 128]]),
                          AP_(dint["wd_bsh"], oc * 128, [[D, FH // 8], [1, 128]]),
                          waits=td if oc == 0 else [],
                          new_group=(oc == 0))
        agl = None
        for oc in range(DCH):
            agl = do_ag(AP_(dint["wd_psh"], oc * (FH // 8) * 128,
                            [[128, FH // 8], [1, 128]]).opt(),
                        AP_(dint["wd_pm"], oc * FH * 128,
                            [[128, FH], [1, 128]]).opt(),
                        [lastc])
        ag["wd"] = agl

    def emit_proj(wt, idim, od, rhs_buf, rhs_off, rhs_stride, tblocks, writer,
                  pan_buf, pan_off, ag_tok, pan_waits=()):
        nich = idim // 128
        npan = od // 256
        bank_i = 0
        pan_cons = [None, None]
        first = True
        for pan in range(npan):
            half = pan % 2
            pofs = pan_off + half * nich * 256
            w0 = [pan_cons[half], ag_tok] + (list(pan_waits) if first else [])
            first = False
            tk = None
            for seg in range(0, nich, 8):
                cnt = min(8, nich - seg)
                tk = p.dma("sync",
                           pan_buf[:, pofs + seg * 256: pofs + (seg + cnt) * 256],
                           AP_(wt, pan * 256 + seg * 128 * od,
                               [[od, 128], [128 * od, cnt], [1, 256]]),
                           waits=w0 if seg == 0 else [],
                           new_group=(seg == 0))
            last_mm = None
            for occ in range(2):
                oc = pan * 2 + occ
                for (tb, t0, tw) in tblocks:
                    bank = bank_i % 2
                    bank_i += 1
                    tmm = None
                    for ic in range(nich):
                        tmm = mm(ps[bank][:, 0:tw],
                                 pan_buf[:, pofs + ic * 256 + occ * 128:
                                         pofs + ic * 256 + occ * 128 + 128],
                                 rhs_buf[:, rhs_off + ic * rhs_stride + t0:
                                         rhs_off + ic * rhs_stride + t0 + tw],
                                 start=(ic == 0), stop=(ic == nich - 1),
                                 waits=([tk] + psw(bank)) if ic == 0 else [],
                                 inc=True if ic == nich - 1 else None)
                    ctok = writer(oc, tb, t0, tw, ps[bank][:, 0:tw], tmm)
                    ptok[bank] = ctok
                    last_mm = tmm
            pan_cons[half] = last_mm

    LS = {}

    def phase_ln(L):
        prev_tsr = None
        for tb in range(2):
            t0 = tb * 512
            tmm1 = None
            for ic in range(DCH):
                tmm1 = mm(ps[2][0:1, 0:512], ONE_COL_F,
                          X[:, ic * T + t0: ic * T + t0 + 512],
                          start=(ic == 0), stop=(ic == DCH - 1),
                          waits=([t_consts] + psw(2)) if ic == 0 else [],
                          inc=True if ic == DCH - 1 else None)
            sq_cons = [None, None]
            tmm2 = None
            for ic in range(DCH):
                half = ic % 2
                sq = SCR[:, half * 512:(half + 1) * 512]
                tsq = ev(lambda e, o=sq, a=X[:, ic * T + t0: ic * T + t0 + 512]:
                         e.tensor_tensor(o, a, a, AO.mult),
                         waits=[sq_cons[half]])
                tmm2 = mm(ps[3][0:1, 0:512], ONE_COL_F, sq,
                          start=(ic == 0), stop=(ic == DCH - 1),
                          waits=[tsq] + (psw(3) if ic == 0 else []),
                          inc=True)
                sq_cons[half] = tmm2
            tme = ev(lambda e, o=ROWS[0:1, R_MEAN + t0:R_MEAN + t0 + 512],
                     i_=ps[2][0:1, 0:512]:
                     e.tensor_single_scalar(o, i_, 1.0 / 2048.0, AO.mult),
                     waits=[tmm1])
            ptok[2] = tme
            tv1 = ev(lambda e, i_=ps[3][0:1, 0:512]:
                     e.tensor_single_scalar(SCR[0:1, 2560:3072], i_, 1.0 / 2048.0,
                                            AO.mult), waits=[tmm2])
            ptok[3] = tv1
            tv2 = ev(lambda e, m=ROWS[0:1, R_MEAN + t0:R_MEAN + t0 + 512]:
                     e.tensor_tensor(SCR[0:1, 2048:2560], m, m, AO.mult),
                     waits=[prev_tsr] if prev_tsr is not None else [])
            tv3 = ev(lambda e: e.tensor_tensor(SCR[0:1, 2560:3072],
                                               SCR[0:1, 2560:3072],
                                               SCR[0:1, 2048:2560], AO.subtract),
                     waits=[])
            tv4 = ev(lambda e: e.tensor_single_scalar(SCR[0:1, 2560:3072],
                                                      SCR[0:1, 2560:3072], EPS,
                                                      AO.add), waits=[])
            tr = ev(lambda e: e.reciprocal(SCR[0:1, 2048:2560], SCR[0:1, 2560:3072]),
                    waits=[])
            tsr = ev(lambda e, o=ROWS[0:1, R_INV + t0:R_INV + t0 + 512]:
                     e.activation(o, SCR[0:1, 2048:2560], AF.Sqrt),
                     waits=[tr], eng="scalar")
            LS[("lnst", tb)] = (tme, tsr)
            prev_tsr = tsr
        for tb in range(2):
            t0 = tb * 512
            tme, tsr = LS[("lnst", tb)]
            tbm = mm(ps[2 + tb][:, 0:512], ONES_ROW,
                     ROWS[0:1, R_MEAN + t0:R_MEAN + t0 + 512],
                     start=True, stop=True, waits=[tme] + psw(2 + tb), inc=True)
            tbi = mm(ps[4 + tb][:, 0:512], ONES_ROW,
                     ROWS[0:1, R_INV + t0:R_INV + t0 + 512],
                     start=True, stop=True, waits=[tsr] + psw(4 + tb), inc=True)
            LS[("lnb", tb)] = (tbm, tbi)
        last = None
        for ic in range(DCH):
            for tb in range(2):
                t0 = tb * 512
                tbm, tbi = LS[("lnb", tb)]
                half = ic % 2
                tmp = SCR[:, half * 512:(half + 1) * 512]
                ev(lambda e, o=tmp, a=X[:, ic * T + t0: ic * T + t0 + 512],
                   b=ps[2 + tb][:, 0:512]:
                   e.tensor_tensor(o, a, b, AO.subtract), waits=[tbm, tbi])
                last = ev(lambda e, o=A[:, ic * T + t0: ic * T + t0 + 512],
                          a=tmp, b=ps[4 + tb][:, 0:512]:
                          e.tensor_tensor(o, a, b, AO.mult), waits=[])
        for b in (2, 3, 4, 5):
            ptok[b] = last
        LS["xn"] = last

    def phase_bvb():
        tld = p.dma("sync", ROWS[0:1, 0:2048], AP_(di["bv"], 0, [[2048, 1], [1, 2048]]))
        toks = []
        for i4 in range(4):
            bank = i4 % 2
            tb1 = mm(ps[bank][:, 0:512], ONES_ROW,
                     ROWS[0:1, i4 * 512:(i4 + 1) * 512],
                     start=True, stop=True,
                     waits=[tld, t_consts] + psw(bank), inc=True)
            tc = ev(lambda e, o=SCR[:, i4 * 512:(i4 + 1) * 512],
                    i_=ps[bank][:, 0:512]: e.tensor_copy(o, i_), waits=[tb1])
            ptok[bank] = tc
            toks.append(tc)
        LS["bvb"] = toks[-1]

    def phase_kv(L):
        xn_tok = LS["xn"]
        kvh = NH if L == 0 else KVH
        kd = kvh * DH
        wkt = dint["wk_t"] if L == 0 else dint["l1k_t"]
        wvt = dint["wv_t"] if L == 0 else dint["l1v_t"]
        ksh = dint["k0_sh"] if L == 0 else dint["k1_sh"]
        vsh = dint["v0_sh"] if L == 0 else dint["v1_sh"]
        bias_k = (C_BIAS + 16) if L == 0 else None
        st_toks = []
        nprev = [None]

        def wr_k(oc, tb, t0, tw, psa, tmm):
            stg = B[:, 24576 + (oc % 2) * 1024 + t0:24576 + (oc % 2) * 1024 + t0 + tw]
            w = [tmm] + ([st_toks[-4]] if len(st_toks) >= 4 else [])
            if bias_k is not None:
                tkk = ev(lambda e, o=stg, i_=psa,
                         bc=CONS[:, bias_k + oc:bias_k + oc + 1]:
                         e.tensor_scalar(o, i_, bc, None, AO.add), waits=w)
            else:
                tkk = ev(lambda e, o=stg, i_=psa: e.tensor_copy(o, i_), waits=w)
            tsq = ev(lambda e, o=QH[:, 0:tw], a=stg:
                     e.tensor_tensor(o, a, a, AO.mult),
                     waits=[tkk] + ([nprev[0]] if nprev[0] else []))
            tnm = mm(ps7[0:1, 0:tw], ONE_COL_B, QH[:, 0:tw],
                     start=True, stop=True, waits=[tsq] + psw(7), inc=True)
            nprev[0] = tnm
            trd = ev(lambda e, o=MISC[0:1, 608 + oc:609 + oc], i_=ps7[0:1, 0:tw]:
                     e.tensor_reduce(o, i_, AX.X, AO.max), waits=[tnm])
            ptok[7] = trd
            if tb == 0:
                ev(lambda e, oc_=oc:
                   e.tensor_copy(SCR[0:1, R_KNL + oc_:R_KNL + oc_ + 1],
                                 MISC[0:1, 608 + oc_:609 + oc_]), waits=[])
            else:
                ev(lambda e, oc_=oc:
                   e.tensor_tensor(SCR[0:1, R_KNL + oc_:R_KNL + oc_ + 1],
                                   MISC[0:1, 608 + oc_:609 + oc_],
                                   SCR[0:1, R_KNL + oc_:R_KNL + oc_ + 1],
                                   AO.max), waits=[])
            tstd = p.dma("gpsimd", ksh[oc * 128:(oc + 1) * 128, t0:t0 + tw], stg,
                         waits=[tkk])
            st_toks.append(tstd)
            return tkk

        emit_proj(wkt, D, kd, A, 0, T, [(0, 0, 512), (1, 512, 512)], wr_k,
                  W, 0, ag["wk" if L == 0 else "l1k"], pan_waits=[xn_tok])
        kag_t = do_ag(ksh.ap().opt(),
                      (dint["k0_ag"] if L == 0 else dint["k1_ag"]).ap().opt(),
                      st_toks)
        tz = ev(lambda e: e.memset(SCR[0:1, R_PACK:R_PACK + 16], 0.0), waits=[])
        tpk = ev(lambda e, k=kvh: e.tensor_copy(MISC[0:1, 624:624 + k],
                                                SCR[0:1, R_KNL:R_KNL + k]),
                 waits=[tz])
        nin = dint["nrm0_in"] if L == 0 else dint["nrm1_in"]
        nout = dint["nrm0_out"] if L == 0 else dint["nrm1_out"]
        tnd = p.dma("gpsimd", nin.ap(), SCR[0:1, R_PACK:R_PACK + 16], waits=[tpk])
        t_ar = do_cc("AllReduce", AO.max, nin.ap().opt(), nout.ap().opt(), [tnd])
        tldk = p.dma("sync", SCR[0:1, R_KMG:R_KMG + 16], nout.ap(), waits=[t_ar])
        LS["knorm"] = tldk

        p.barrier()
        vst = []
        pan_cons = [None, None]
        for pan in range(kd // 256):
            half = pan % 2
            pofs = half * DCH * 256
            tk = None
            for seg in range(0, DCH, 8):
                tk = p.dma("sync", W[:, pofs + seg * 256: pofs + (seg + 8) * 256],
                           AP_(wvt, pan * 256 + seg * 128 * kd,
                               [[kd, 128], [128 * kd, 8], [1, 256]]),
                           waits=[pan_cons[half], ag["wv" if L == 0 else "l1v"]]
                           if seg == 0 else [],
                           new_group=(seg == 0))
            last_mm = None
            for tch in range(8):
                bank = tch % 2
                tmm = None
                for ic in range(DCH):
                    tmm = mm(ps[bank][:, 0:256],
                             A[:, ic * T + tch * 128: ic * T + (tch + 1) * 128],
                             W[:, pofs + ic * 256:(pofs + (ic + 1) * 256)],
                             start=(ic == 0), stop=(ic == DCH - 1),
                             waits=([tk, xn_tok] + psw(bank)) if ic == 0 else [],
                             inc=True if ic == DCH - 1 else None)
                stg = B[:, 26624 + (tch % 4) * 512: 26624 + (tch % 4) * 512 + 256]
                wv_w = [tmm] + (vst[-8:-6] if len(vst) >= 8 else [])
                if L == 0:
                    tkv = ev(lambda e, o=stg, i_=ps[bank][:, 0:256],
                             b=SCR[:, pan * 256:(pan + 1) * 256]:
                             e.tensor_tensor(o, i_, b, AO.add),
                             waits=wv_w + [LS["bvb"]])
                else:
                    tkv = ev(lambda e, o=stg, i_=ps[bank][:, 0:256]:
                             e.tensor_copy(o, i_), waits=wv_w)
                ptok[bank] = tkv
                last_mm = tmm
                for hh in range(2):
                    h = pan * 2 + hh
                    tstd = p.dma("gpsimd",
                                 vsh[h * T + tch * 128: h * T + (tch + 1) * 128, :],
                                 stg[:, hh * 128:(hh + 1) * 128], waits=[tkv])
                    vst.append(tstd)
            pan_cons[half] = last_mm
        vag_t = do_ag(vsh.ap().opt(),
                      (dint["v0_ag"] if L == 0 else dint["v1_ag"]).ap().opt(),
                      vst)
        LS["kv_ag"] = (kag_t, vag_t)

    def sparsify_group(ch0, tprev, hid_waits=()):
        def h_(r):
            return SCR[:, r * 256:(r + 1) * 256]

        def s_(k):
            if k < 4:
                return SCR[:, 2048 + k * 256:2048 + (k + 1) * 256]
            return SCR[:, 1024 + (k - 4) * 256:1024 + (k - 3) * 256]
        ev(lambda e: e.tensor_tensor(s_(0), h_(0), h_(0), AO.mult), waits=[tprev])
        ev(lambda e: e.tensor_tensor(s_(1), h_(1), h_(1), AO.mult), waits=[])
        ev(lambda e: e.tensor_tensor(s_(2), h_(2), h_(2), AO.mult), waits=[])
        ev(lambda e: e.tensor_tensor(s_(3), h_(3), h_(3), AO.mult), waits=[])
        ev(lambda e: e.tensor_tensor(s_(4), s_(0), s_(1), AO.min), waits=[])
        ev(lambda e: e.tensor_tensor(s_(5), s_(2), s_(3), AO.min), waits=[])
        ev(lambda e: e.tensor_tensor(s_(0), s_(0), s_(1), AO.max), waits=[])
        ev(lambda e: e.tensor_tensor(s_(2), s_(2), s_(3), AO.max), waits=[])
        ev(lambda e: e.tensor_tensor(s_(0), s_(0), s_(2), AO.min), waits=[])
        ev(lambda e: e.tensor_tensor(s_(4), s_(4), s_(5), AO.max), waits=[])
        ev(lambda e: e.tensor_tensor(s_(0), s_(0), s_(4), AO.max), waits=[])
        outs = None
        for r in range(4):
            ev(lambda e, r_=r: e.tensor_tensor(s_(1), h_(r_), h_(r_), AO.mult),
               waits=[])
            ev(lambda e: e.tensor_tensor(s_(1), s_(1), s_(0), AO.is_ge), waits=[])
            outs = ev(lambda e, r_=r, c0=ch0:
                      e.tensor_tensor(B[:, (c0 + r_) * TBF:(c0 + r_ + 1) * TBF],
                                      h_(r_), s_(1), AO.mult),
                      waits=hid_waits)
        return outs

    def fwht_stages(tprev):
        last = tprev
        for ch2 in range(FCH // 2):
            bank = ch2 % 2
            tmm = None
            for j in range(2):
                ch = ch2 * 2 + j
                tmm = mm(ps[bank][:, j * 256:(j + 1) * 256],
                         CONB[:, B_H128:B_H128 + 128],
                         B[:, ch * TBF:(ch + 1) * TBF],
                         start=True, stop=True,
                         waits=([tprev, t_consts] + psw(bank)) if j == 0 else [],
                         inc=True if j == 1 else None)
            tcp = ev(lambda e, o=B[:, ch2 * 512:(ch2 + 1) * 512],
                     i_=ps[bank][:, 0:512]: e.tensor_copy(o, i_), waits=[tmm])
            ptok[bank] = tcp
            last = tcp
        srcb, dstb = 0, FCH * TBF
        h = 1
        while h < 64:
            nb = 64 // (2 * h)
            ev(lambda e,
               o=AP_(B.ap().tensor, dstb, [[BP, 128], [2 * h * TBF, nb], [1, h * TBF]]),
               a=AP_(B.ap().tensor, srcb, [[BP, 128], [2 * h * TBF, nb], [1, h * TBF]]),
               b=AP_(B.ap().tensor, srcb + h * TBF,
                     [[BP, 128], [2 * h * TBF, nb], [1, h * TBF]]):
               e.tensor_tensor(o, a, b, AO.add), waits=[last])
            last = ev(lambda e,
                      o=AP_(B.ap().tensor, dstb + h * TBF,
                            [[BP, 128], [2 * h * TBF, nb], [1, h * TBF]]),
                      a=AP_(B.ap().tensor, srcb,
                            [[BP, 128], [2 * h * TBF, nb], [1, h * TBF]]),
                      b=AP_(B.ap().tensor, srcb + h * TBF,
                            [[BP, 128], [2 * h * TBF, nb], [1, h * TBF]]):
                      e.tensor_tensor(o, a, b, AO.subtract), waits=[])
            srcb, dstb = dstb, srcb
            h *= 2
        return last

    def phase_ffn(L):
        xn_tok = LS["xn"]
        w_g = dint["w1_t"] if L == 0 else dint["wg_b"]
        w_u = dint["w3_t"] if L == 0 else dint["wu_b"]
        wpan = dint["w2_pm"] if L == 0 else dint["wd_pm"]
        agg = ag["w1" if L == 0 else "wg"]
        agu = ag["w3" if L == 0 else "wu"]
        agd = ag["w2" if L == 0 else "wd"]
        PANR = FCH * TBF
        down_cons = [None, None]
        for tb in range(NTBF):
            t0 = tb * TBF
            pan_cons_B = [down_cons[0], down_cons[1]]
            pan_cons_W = [None, None]
            dprev = [t for t in down_cons if t is not None]
            hid_last = None
            l1last = None
            last_th = None
            for pan in range(FH // 256):
                half = pan % 2
                pofs = PANR + half * DCH * 256
                uofs = half * DCH * 256
                tkg = tku = None
                for seg in range(0, DCH, 8):
                    tkg = p.dma("sync",
                                B[:, pofs + seg * 256: pofs + (seg + 8) * 256],
                                AP_(w_g, pan * 256 + seg * 128 * FH,
                                    [[FH, 128], [128 * FH, 8], [1, 256]]),
                                waits=[pan_cons_B[half], agg] if seg == 0 else [],
                                new_group=(seg == 0))
                for seg in range(0, DCH, 8):
                    tku = p.dma("scalar",
                                W[:, uofs + seg * 256: uofs + (seg + 8) * 256],
                                AP_(w_u, pan * 256 + seg * 128 * FH,
                                    [[FH, 128], [128 * FH, 8], [1, 256]]),
                                waits=[pan_cons_W[half], agu] if seg == 0 else [],
                                new_group=(seg == 0))
                for occ in range(2):
                    ch = pan * 2 + occ
                    bg = occ * 2
                    bu = occ * 2 + 1
                    tgm = tum = None
                    for ic in range(DCH):
                        tgm = mm(ps[bg][:, 0:TBF],
                                 B[:, pofs + ic * 256 + occ * 128:
                                   pofs + ic * 256 + occ * 128 + 128],
                                 A[:, ic * T + t0: ic * T + t0 + TBF],
                                 start=(ic == 0), stop=(ic == DCH - 1),
                                 waits=([tkg, xn_tok] + psw(bg)) if ic == 0 else [],
                                 inc=True if ic == DCH - 1 else None)
                    for ic in range(DCH):
                        tum = mm(ps[bu][:, 0:TBF],
                                 W[:, uofs + ic * 256 + occ * 128:
                                   uofs + ic * 256 + occ * 128 + 128],
                                 A[:, ic * T + t0: ic * T + t0 + TBF],
                                 start=(ic == 0), stop=(ic == DCH - 1),
                                 waits=([tku] + psw(bu)) if ic == 0 else [],
                                 inc=True if ic == DCH - 1 else None)
                    pan_cons_B[half] = tgm
                    pan_cons_W[half] = tum
                    tsl = ev(lambda e, o=QH[:, 0:TBF], i_=ps[bg][:, 0:TBF]:
                             e.activation(o, i_, AF.Silu),
                             waits=[tgm] + ([last_th] if last_th else []),
                             eng="scalar")
                    ptok[bg] = tsl
                    if L == 0:
                        th = ev(lambda e, o=B[:, ch * TBF:(ch + 1) * TBF],
                                u_=ps[bu][:, 0:TBF], g_=QH[:, 0:TBF]:
                                e.scalar_tensor_tensor(o, u_, INV_SQRT_FH, g_,
                                                       AO.mult, AO.mult),
                                waits=[tsl, tum] + dprev)
                        ptok[bu] = th
                        hid_last = th
                        last_th = th
                    else:
                        gi = ch % 4
                        th = ev(lambda e, o=SCR[:, gi * 256:(gi + 1) * 256],
                                u_=ps[bu][:, 0:TBF], g_=QH[:, 0:TBF]:
                                e.scalar_tensor_tensor(o, u_, 1.0, g_,
                                                       AO.mult, AO.mult),
                                waits=[tsl, tum])
                        ptok[bu] = th
                        l1last = th
                        last_th = th
                        if gi == 3:
                            hid_last = sparsify_group(ch - 3, l1last,
                                                      hid_waits=dprev)
            if L == 0:
                ht = fwht_stages(hid_last)
            else:
                ht = hid_last
            for oc in range(DCH):
                half = oc % 2
                dofs = PANR + half * FCH * 128
                tk = p.dma("sync", B[:, dofs:dofs + FCH * 128],
                           AP_(wpan, oc * FH * 128,
                               [[128, 128], [128 * 128, FCH], [1, 128]]),
                           waits=[down_cons[half], ht, agd])
                bank = 4 + half
                tmm = None
                for ic in range(FCH):
                    tmm = mm(ps[bank][:, 0:TBF],
                             B[:, dofs + ic * 128: dofs + (ic + 1) * 128],
                             B[:, ic * TBF:(ic + 1) * TBF],
                             start=(ic == 0), stop=(ic == FCH - 1),
                             waits=([tk, ht] + psw(bank)) if ic == 0 else [],
                             inc=True if ic == FCH - 1 else None)
                tadd = ev(lambda e, o=X[:, oc * T + t0: oc * T + t0 + TBF],
                          i_=ps[bank][:, 0:TBF]:
                          e.tensor_tensor(o, i_, o, AO.add), waits=[tmm])
                ptok[bank] = tadd
                down_cons[half] = tmm
        p.barrier()

    def phase_q(L):
        xn_tok = LS["xn"]
        wqt = dint["wq_t"] if L == 0 else dint["l1q_t"]
        bias_q = C_BIAS if L == 0 else None
        tqp = p.dma("sync", ROWS[0:1, R_INV:R_INV + T],
                    AP_(di["qpos"], 0, [[T, 1], [1, T]]))
        LS["qpos"] = tqp
        nprev = [None]

        def wr_q(oc, tb, t0, tw, psa, tmm):
            dst = B[:, oc * T + t0: oc * T + t0 + tw]
            if bias_q is not None:
                t1 = ev(lambda e, o=dst, i_=psa,
                        bc=CONS[:, bias_q + oc:bias_q + oc + 1]:
                        e.tensor_scalar(o, i_, bc, SCALE, AO.add, AO.mult),
                        waits=[tmm])
            else:
                t1 = ev(lambda e, o=dst, i_=psa:
                        e.tensor_single_scalar(o, i_, SCALE, AO.mult), waits=[tmm])
            tsq = ev(lambda e, o=QH[:, 0:tw], a=dst:
                     e.tensor_tensor(o, a, a, AO.mult),
                     waits=[t1] + ([nprev[0]] if nprev[0] else []))
            tnm = mm(ps7[0:1, 0:tw], ONE_COL_B, QH[:, 0:tw],
                     start=True, stop=True, waits=[tsq] + psw(7), inc=True)
            nprev[0] = tnm
            trd = ev(lambda e, o=MISC[0:1, 608 + oc:609 + oc], i_=ps7[0:1, 0:tw]:
                     e.tensor_reduce(o, i_, AX.X, AO.max), waits=[tnm])
            ptok[7] = trd
            if tb == 0:
                ev(lambda e, oc_=oc:
                   e.tensor_copy(SCR[0:1, R_QM + oc_:R_QM + oc_ + 1],
                                 MISC[0:1, 608 + oc_:609 + oc_]), waits=[])
            else:
                ev(lambda e, oc_=oc:
                   e.tensor_tensor(SCR[0:1, R_QM + oc_:R_QM + oc_ + 1],
                                   MISC[0:1, 608 + oc_:609 + oc_],
                                   SCR[0:1, R_QM + oc_:R_QM + oc_ + 1],
                                   AO.max), waits=[])
            return t1

        emit_proj(wqt, D, D, A, 0, T, [(0, 0, 512), (1, 512, 512)], wr_q,
                  W, 0, ag["wq" if L == 0 else "l1q"], pan_waits=[xn_tok])
        tkn = LS["knorm"]
        if L == 0:
            tcm = ev(lambda e: e.tensor_tensor(SCR[0:1, R_PRD:R_PRD + 16],
                                               SCR[0:1, R_QM:R_QM + 16],
                                               SCR[0:1, R_KMG:R_KMG + 16],
                                               AO.mult), waits=[tkn])
        else:
            tcm = ev(lambda e: e.tensor_tensor(
                AP_(SCR.ap().tensor, R_PRD, [[4096, 1], [4, 4], [1, 4]]),
                AP_(SCR.ap().tensor, R_QM, [[4096, 1], [4, 4], [1, 4]]),
                AP_(SCR.ap().tensor, R_KMG, [[4096, 1], [1, 4], [0, 4]]),
                AO.mult), waits=[tkn])
        tsq2 = ev(lambda e: e.activation(SCR[0:1, R_SQ:R_SQ + 16],
                                         SCR[0:1, R_PRD:R_PRD + 16], AF.Sqrt),
                  waits=[tcm], eng="scalar")
        tcp = ev(lambda e: e.tensor_scalar(SCR[0:1, R_CP:R_CP + 16],
                                           SCR[0:1, R_SQ:R_SQ + 16],
                                           1.05, 2.0, AO.mult, AO.add),
                 waits=[tsq2])
        LS["Cpos"] = tcp
        if L == 0:
            tneg = ev(lambda e: e.tensor_single_scalar(SCR[0:1, R_CN:R_CN + 16],
                                                       SCR[0:1, R_CP:R_CP + 16],
                                                       -1.0, AO.mult), waits=[tcp])
            tbc = mm(ps7[:, 0:16], ONES_ROW, SCR[0:1, R_CN:R_CN + 16],
                     start=True, stop=True, waits=[tneg] + psw(7), inc=True)
            tcb = ev(lambda e: e.tensor_copy(MISC[:, 256:272], ps7[:, 0:16]),
                     waits=[tbc])
            ptok[7] = tcb
            LS["Cneg_b"] = tcb

    def mask_ap(w):
        if w < 12:
            return SCR[:, w * 256:(w + 1) * 256]
        return QH[:, (w - 12) * 256:(w - 11) * 256]

    def build_masks(i):
        t0 = i * QB
        tbq = mm(ps7[:, 0:QB], ONES_ROW, ROWS[0:1, R_INV + t0:R_INV + t0 + QB],
                 start=True, stop=True,
                 waits=[LS["qpos"], t_consts] + psw(7), inc=True)
        tqb = ev(lambda e: e.tensor_copy(MISC[:, 0:QB], ps7[:, 0:QB]), waits=[tbq])
        ptok[7] = tqb
        last = None
        for w in range(16):
            cc, bb = w // 2, w % 2
            kb = cc * 8 + 2 * i + bb
            last = ev(lambda e, o=mask_ap(w),
                      kc=CONS[:, C_KPOS + kb:C_KPOS + kb + 1]:
                      e.tensor_scalar(o, MISC[:, 0:QB], kc, 0.0,
                                      AO.subtract, AO.min), waits=[tqb])
        return last

    attn_ctr = [0]
    pend_tail = [None]

    def attn_tile(L, h, i, tks, tvs, tmask, nkb, prev_out, half_tok):
        ATT = DCH * T
        t0 = i * QB
        kbs = (list(range(64)) if L == 0 else
               [cc * 8 + b for cc in range(8) for b in range(2 * (i + 1))])
        ngr = len(kbs) // 4
        bias_path = (L == 1 and SLOPES[h] * 255.0 < 55.0)
        grpsz = 1 if bias_path else 4
        exp_toks = []
        if L == 1 and not bias_path:
            tbch = ev(lambda e, sl=SLOPES[h]:
                      e.tensor_scalar(MISC[:, 272:273], CONS[:, C_KPOS:C_KPOS + 1],
                                      sl, None, AO.mult),
                      waits=[t_consts] + ([prev_out[0]] if prev_out else []))
        if bias_path:
            tq1 = ev(lambda e, sl=SLOPES[h]:
                     e.tensor_scalar(SCR[0:1, 3468:3469],
                                     ROWS[0:1, R_INV + t0 + QB - 1:R_INV + t0 + QB],
                                     sl, None, AO.mult),
                     waits=[LS["qpos"]])
            tq2 = ev(lambda e, hh=h:
                     e.tensor_scalar(SCR[0:1, 3468:3469], SCR[0:1, 3468:3469],
                                     SCR[0:1, R_CP + hh:R_CP + hh + 1], None,
                                     AO.add), waits=[LS["Cpos"]])
            tqb2 = mm(ps7[:, 0:1], ONES_ROW, SCR[0:1, 3468:3469],
                      start=True, stop=True, waits=[tq2] + psw(7), inc=True)
            tcsb = ev(lambda e: e.tensor_copy(MISC[:, 280:281], ps7[:, 0:1]),
                      waits=[tqb2] + ([prev_out[0]] if prev_out else []))
            ptok[7] = tcsb
            tcsb = ev(lambda e, sl=SLOPES[h]:
                      e.tensor_scalar(MISC[:, 288:352], CONS[:, C_KPOS:C_KPOS + 64],
                                      sl, MISC[:, 280:281], AO.mult, AO.subtract),
                      waits=[t_consts])
        qreg = [(psA, 0, 0), (psA, 512, 1), (psB, 0, 2), (psB, 512, 3)]
        gr_key = [(0, 1), (2, 3)]
        close_prev = [None]

        def emit_group(g):
            slots = [g * 4 + j for j in range(4)]
            if bias_path:
                for j in range(4):
                    n_ = slots[j]
                    kb = kbs[n_]
                    pgq, qo, qkey = qreg[n_ % 4]
                    if n_ % 4 == 0:
                        w0 = [tks] + psw(0) + psw(1) + psw(2) + psw(3)
                    else:
                        w0 = []
                    tmm = mm(pgq[:, qo:qo + QB], A[:, kb * 128:(kb + 1) * 128],
                             B[:, h * T + t0: h * T + t0 + QB],
                             start=True, stop=True, waits=w0, inc=True)
                    tpre = tmm
                    if (kb % 8) >= 2 * i:
                        w_ = (kb // 8) * 2 + (kb % 8) - 2 * i
                        tpre = ev(lambda e, o=pgq[:, qo:qo + QB], m=mask_ap(w_):
                                  e.scalar_tensor_tensor(o, m, BIGM, o,
                                                         AO.mult, AO.add),
                                  waits=[tmm, tmask])
                    ew = [tpre, tcsb]
                    if n_ == 0:
                        ew += [half_tok] + ([prev_out[1]] if prev_out else [])
                    te = ev(lambda e, o=B[:, ATT + n_ * QB:ATT + (n_ + 1) * QB],
                            i_=pgq[:, qo:qo + QB], b=MISC[:, 288 + kb:289 + kb]:
                            e.activation(o, i_, AF.Exp, bias=b),
                            waits=ew, eng="scalar")
                    ptok[qkey] = te
                    exp_toks.append(te)
                return
            pg = [psA, psB][g % 2]
            key = gr_key[g % 2]
            w0 = [tks] + psw(key[0]) + psw(key[1])
            if L == 1:
                tcr = None
                for j in range(4):
                    kb = kbs[slots[j]]
                    offc = float(-SLOPES[h] * 128.0 * KB_GBLK[kb])
                    ev(lambda e, j_=j, sl=SLOPES[h], of=offc:
                       e.tensor_scalar(ROWS[0:1, j_ * 256:(j_ + 1) * 256],
                                       ROWS[0:1, R_INV + t0:R_INV + t0 + QB],
                                       sl, of, AO.mult, AO.add),
                       waits=[LS["qpos"]] +
                       ([close_prev[0]] if (j == 0 and close_prev[0]) else []))
                    tcr = ev(lambda e, j_=j, hh=h:
                             e.tensor_scalar(ROWS[0:1, j_ * 256:(j_ + 1) * 256],
                                             ROWS[0:1, j_ * 256:(j_ + 1) * 256],
                                             SCR[0:1, R_CP + hh:R_CP + hh + 1],
                                             None, AO.add),
                             waits=[LS["Cpos"]])
                tclose = None
                for hf in range(2):
                    mm(pg[:, hf * 512:(hf + 1) * 512], NEG_ROW,
                       ROWS[0:1, hf * 512:(hf + 1) * 512],
                       start=True, stop=False,
                       waits=(w0 if hf == 0 else []) + [tcr])
                    for j in (hf * 2, hf * 2 + 1):
                        kb = kbs[slots[j]]
                        tclose = mm(pg[:, j * QB:(j + 1) * QB],
                                    A[:, kb * 128:(kb + 1) * 128],
                                    B[:, h * T + t0: h * T + t0 + QB],
                                    start=False, stop=(j == hf * 2 + 1),
                                    waits=[],
                                    inc=True if j == hf * 2 + 1 else None)
            else:
                tclose = None
                for j in range(4):
                    kb = kbs[slots[j]]
                    tclose = mm(pg[:, j * QB:(j + 1) * QB],
                                A[:, kb * 128:(kb + 1) * 128],
                                B[:, h * T + t0: h * T + t0 + QB],
                                start=True, stop=True,
                                waits=(w0 if j == 0 else []),
                                inc=True if j == 3 else None)
            close_prev[0] = tclose
            tpre = tclose
            if L == 1:
                for j in range(4):
                    kb = kbs[slots[j]]
                    if (kb % 8) >= 2 * i:
                        w_ = (kb // 8) * 2 + (kb % 8) - 2 * i
                        tpre = ev(lambda e, o=pg[:, j * QB:(j + 1) * QB],
                                  m=mask_ap(w_):
                                  e.scalar_tensor_tensor(o, m, BIGM, o,
                                                         AO.mult, AO.add),
                                  waits=[tclose, tmask])
            bias_ap = (MISC[:, 256 + h:257 + h] if L == 0 else MISC[:, 272:273])
            bw = [tpre] + ([LS["Cneg_b"]] if L == 0 else [tbch])
            if g == 0:
                bw += [half_tok] + ([prev_out[1]] if prev_out else [])
            te = ev(lambda e, o=B[:, ATT + g * 1024:ATT + (g + 1) * 1024],
                    i_=pg[:, 0:1024], b=bias_ap:
                    e.activation(o, i_, AF.Exp, bias=b), waits=bw, eng="scalar")
            ptok[key[0]] = te
            ptok[key[1]] = te
            exp_toks.append(te)

        red_state = {"den": None, "av": None}

        def emit_red(g):
            first = (g == 0)
            last = (g == ngr - 1)
            for pp in range(2):
                p2 = g * 2 + pp
                wde = [exp_toks[(2 * p2 + 1) // grpsz]]
                if p2 == 0:
                    wde += psw(5)
                red_state["den"] = mm(psD[0:1, 0:512], ONE_COL_B,
                                      B[:, ATT + p2 * 512:ATT + (p2 + 1) * 512],
                                      start=(p2 == 0), stop=(last and pp == 1),
                                      waits=wde,
                                      inc=True if (last and pp == 1) else None)
            for j in range(4):
                n_ = g * 4 + j
                kb = kbs[n_]
                wav = []
                if n_ == 0:
                    wav = [tvs] + psw(4)
                red_state["av"] = mm(psC[:, 0:QB],
                                     A[:, 8192 + kb * 128: 8192 + (kb + 1) * 128],
                                     B[:, ATT + n_ * QB:ATT + (n_ + 1) * QB],
                                     start=(n_ == 0), stop=(last and j == 3),
                                     waits=wav,
                                     inc=True if (last and j == 3) else None)

        for g in range(ngr):
            emit_group(g)
            if g == 0 and pend_tail[0] is not None:
                pend_tail[0]()
                pend_tail[0] = None
            if g >= 2:
                emit_red(g - 2)
        emit_red(ngr - 2)
        emit_red(ngr - 1)
        tden = red_state["den"]
        tav = red_state["av"]
        par = attn_ctr[0] % 2
        attn_ctr[0] += 1
        frow = 3584 + par * 256
        ev(lambda e, fr=frow: e.tensor_copy(SCR[0:1, fr:fr + 256],
                                            psD[0:1, 0:256]), waits=[tden])
        tfd = ev(lambda e, fr=frow: e.tensor_tensor(SCR[0:1, fr:fr + 256],
                                                    SCR[0:1, fr:fr + 256],
                                                    psD[0:1, 256:512], AO.add),
                 waits=[])
        ptok[5] = tfd

        def tail(_tfd=tfd, _tav=tav, _h=h, _i=i, _ht=half_tok, _fr=frow):
            trc = ev(lambda e: e.reciprocal(SCR[0:1, R_RCP:R_RCP + QB],
                                            SCR[0:1, _fr:_fr + 256]),
                     waits=[_tfd])
            tbc2 = mm(ps7[:, 0:QB], ONES_ROW, SCR[0:1, R_RCP:R_RCP + QB],
                      start=True, stop=True, waits=[trc] + psw(7), inc=True)
            trb = ev(lambda e: e.copy(MISC[:, 352:352 + QB], ps7[:, 0:QB]),
                     waits=[tbc2], eng="scalar")
            ptok[7] = trb
            tno = ev(lambda e, o=W[:, _h * 512 + (_i % 2) * QB:
                                   _h * 512 + (_i % 2) * QB + QB],
                     a=psC[:, 0:QB], b=MISC[:, 352:352 + QB]:
                     e.tensor_tensor(o, a, b, AO.mult),
                     waits=[_tav, trb, _ht])
            ptok[4] = tno
            return tno

        pend_tail[0] = tail
        return (exp_toks[-1], tav, tail)

    def oproj_half(L, half, last_norm):
        wot = dint["wo_t"] if L == 0 else dint["l1o_t"]
        bias_o = (C_BIAS + 48) if L == 0 else None
        t0 = half * 512
        res = [None]

        def wr_o(oc, tb, tt0, tw, psa, tmm):
            dst = X[:, oc * T + t0: oc * T + t0 + 512]
            if bias_o is not None:
                t1 = ev(lambda e, o=dst, i_=psa,
                        bc=CONS[:, bias_o + oc:bias_o + oc + 1]:
                        e.scalar_tensor_tensor(o, i_, bc, o, AO.add, AO.add),
                        waits=[tmm])
            else:
                t1 = ev(lambda e, o=dst, i_=psa:
                        e.tensor_tensor(o, i_, o, AO.add), waits=[tmm])
            res[0] = t1
            return t1

        emit_proj(wot, D, D, W, 0, 512, [(0, 0, 512)], wr_o,
                  B, DCH * T, ag["wo" if L == 0 else "l1o"],
                  pan_waits=[last_norm])
        return res[0]

    def phase_attn(L):
        kag = dint["k0_ag"] if L == 0 else dint["k1_ag"]
        vag = dint["v0_ag"] if L == 0 else dint["v1_ag"]
        kd = D if L == 0 else 512
        nvh = NH if L == 0 else KVH
        tk_ag, tv_ag = LS["kv_ag"]
        slab = {"kh": -1, "vh": -1, "k": None, "v": None,
                "kcons": None, "vcons": None}

        def load_k(h):
            kh = h if L == 0 else h // 4
            if slab["kh"] == kh:
                return slab["k"]
            tkk = None
            for cc in range(8):
                tkk = p.dma("scalar", A[:, cc * 1024:(cc + 1) * 1024],
                            kag[cc * kd + kh * 128: cc * kd + (kh + 1) * 128, :],
                            waits=[tk_ag, slab["kcons"]] if cc == 0 else [],
                            new_group=(cc == 0))
            slab["kh"] = kh
            slab["k"] = tkk
            return tkk

        def load_v(h):
            vh = h if L == 0 else h // 4
            if slab["vh"] == vh:
                return slab["v"]
            tvv = None
            for cc in range(8):
                tvv = p.dma("sync", A[:, 8192 + cc * 1024: 8192 + (cc + 1) * 1024],
                            AP_(vag, (cc * nvh + vh) * T * DH,
                                [[DH, 128], [128 * DH, 8], [1, DH]]),
                            waits=[tv_ag, slab["vcons"]] if cc == 0 else [],
                            new_group=(cc == 0))
            slab["vh"] = vh
            slab["v"] = tvv
            return tvv

        half_tok = None

        def run_tile(h, i, tks, tvs, tmask, prev, ht):
            return attn_tile(L, h, i, tks, tvs, tmask, 0, prev, ht)

        for half in range(2):
            outs = []
            if L == 0:
                for h in range(NH):
                    tks = load_k(h)
                    tvs = load_v(h)
                    for i in ((0, 1) if half == 0 else (2, 3)):
                        o = run_tile(h, i, tks, tvs, None,
                                     outs[-1] if outs else None, half_tok)
                        outs.append(o)
                    slab["kcons"] = o[0]
                    slab["vcons"] = o[1]
                    slab["kh"] = -1 if h == NH - 1 else slab["kh"]
                    slab["vh"] = -1 if h == NH - 1 else slab["vh"]
            else:
                for i in ((0, 1) if half == 0 else (2, 3)):
                    tmask = build_masks(i)
                    for h in range(NH):
                        tks = load_k(h)
                        tvs = load_v(h)
                        o = run_tile(h, i, tks, tvs, tmask,
                                     outs[-1] if outs else None, half_tok)
                        outs.append(o)
                        slab["kcons"] = o[0]
                        slab["vcons"] = o[1]
                    slab["kh"] = -1
                    slab["vh"] = -1
            last_tno = pend_tail[0]()
            pend_tail[0] = None
            half_tok = oproj_half(L, half, last_tno)
        p.barrier()

    prep_one("wk")
    prep_one("wv")
    phase_ln(0)
    p.barrier()
    phase_bvb()
    phase_kv(0)
    prep_rest()
    p.barrier()
    phase_ffn(0)
    phase_q(0)
    p.barrier()
    phase_attn(0)
    phase_ln(1)
    p.barrier()
    phase_kv(1)
    p.barrier()
    phase_ffn(1)
    phase_q(1)
    p.barrier()
    phase_attn(1)

    scr_cons2 = None
    for tch in range(8):
        for grp in range(4):
            bank = grp % 2
            tmm = None
            for j in range(4):
                dcb = grp * 4 + j
                tmm = p.emit(
                    "tensor",
                    lambda e, o=ps[bank][:, j * 128:(j + 1) * 128],
                    i_=X[:, dcb * T + tch * 128: dcb * T + (tch + 1) * 128],
                    idn=CONS[:, C_ID:C_ID + 128]: e.transpose(o, i_, idn),
                    waits=psw(bank) if j == 0 else [],
                    inc=True if j == 3 else None)
            tcp = ev(lambda e, o=SCR[:, grp * 512:(grp + 1) * 512],
                     i_=ps[bank][:, 0:512]: e.tensor_copy(o, i_),
                     waits=[tmm, scr_cons2])
            ptok[bank] = tcp
        tst = p.dma("sync", y[tch * 128:(tch + 1) * 128, :], SCR[:, 0:2048],
                    waits=[ptok[0], ptok[1]])
        scr_cons2 = tst
    for q, eng in (("sync", "sync"), ("gpsimd", "gpsimd"), ("scalar", "scalar")):
        p.wait_only(eng, [(sm, sm.n) for sm in p.dq[q]["ring"] if sm.n > 0])

    blk = nc.Block()
    blk.__enter__()
    for eng in ENGS:
        getattr(blk, eng)(lambda e, _ops=tuple(p.ops[eng]): [f(e) for f in _ops])
    blk.__exit__(None, None, None)
    for g in reversed(ctxs):
        g.__exit__(None, None, None)
    return nc


def kernel(**inputs):
    I = {k: np.asarray(v) for k, v in inputs.items()}
    Xall = np.ascontiguousarray(I["L0_attn_X"].reshape(S, D))
    wdp = np.ascontiguousarray(I["L1_ffn_w_down"][PERM, :])
    wgp = np.ascontiguousarray(I["L1_ffn_w_gate"][:, PERM])
    wup = np.ascontiguousarray(I["L1_ffn_w_up"][:, PERM])
    in_maps = []
    for c in range(NCORES):
        idx = tok_idx(c)
        s0, s1 = c * 256, (c + 1) * 256
        m = {
            "x": np.ascontiguousarray(Xall[idx]),
            "qpos": np.ascontiguousarray(idx.astype(np.float32)),
            "wq_sh": np.ascontiguousarray(I["L0_attn_Wq"][:, s0:s1]),
            "wk_sh": np.ascontiguousarray(I["L0_attn_Wk"][:, s0:s1]),
            "wv_sh": np.ascontiguousarray(I["L0_attn_Wv"][:, s0:s1]),
            "wo_sh": np.ascontiguousarray(I["L0_attn_Wo"][:, s0:s1]),
            "w1_sh": np.ascontiguousarray(I["L0_ffn_W1"][:, s0:s1]),
            "w3_sh": np.ascontiguousarray(I["L0_ffn_W3"][:, s0:s1]),
            "w2_sh": np.ascontiguousarray(I["L0_ffn_W2"][:, c * 1024:(c + 1) * 1024]),
            "l1q_sh": np.ascontiguousarray(I["L1_attn_q_w"][:, s0:s1]),
            "l1k_sh": np.ascontiguousarray(I["L1_attn_k_w"][:, s0:s1]),
            "l1v_sh": np.ascontiguousarray(I["L1_attn_v_w"][:, s0:s1]),
            "l1o_sh": np.ascontiguousarray(I["L1_attn_o_w"][:, s0:s1]),
            "wg_sh": np.ascontiguousarray(wgp[s0:s1, :]),
            "wu_sh": np.ascontiguousarray(wup[s0:s1, :]),
            "wd_sh": np.ascontiguousarray(wdp[c * 1024:(c + 1) * 1024, :]),
            "bq": np.ascontiguousarray(I["L0_attn_bq"]),
            "bk": np.ascontiguousarray(I["L0_attn_bk"]),
            "bv": np.ascontiguousarray(I["L0_attn_bv"]),
            "bo": np.ascontiguousarray(I["L0_attn_bo"]),
        }
        in_maps.append(m)
    nc = build_nc()
    res = run_bass_kernel_spmd(nc, in_maps, core_ids=list(range(NCORES)))
    out = np.empty((S, D), np.float32)
    for c in range(NCORES):
        out[tok_idx(c)] = res.results[c]["y"]
    return out.reshape(1, S, D)


# version 14 (final)
# speedup vs baseline: 1.0136x; 1.0136x over previous
import math
import numpy as np
import ml_dtypes
import concourse.bass as bass
import concourse.mybir as mybir
from concourse.bass_utils import run_bass_kernel_spmd

F32 = mybir.dt.float32
BF16 = mybir.dt.bfloat16
AF = mybir.ActivationFunctionType
AO = mybir.AluOpType
AX = mybir.AxisListType
BFD = ml_dtypes.bfloat16

NCORES = 8
S, D, T = 8192, 2048, 1024
NH, DH, KVH = 16, 128, 4
FH = 8192
DCH = D // 128
FCH = FH // 128
TBF = 256
NTBF = T // TBF
QB = 256
SCALE = float(1.0 / np.sqrt(128.0))
INV_SQRT_FH = float(1.0 / np.sqrt(float(FH)))
EPS = 1e-5
BIGM = 60000.0


def blocks_of(c):
    return [16 * i + 2 * c + j for i in range(4) for j in range(2)]


def tok_idx(c):
    return np.concatenate([np.arange(g * 128, (g + 1) * 128) for g in blocks_of(c)])


PERM = np.empty(FH, np.int64)
for _m in range(64):
    _q, _r = _m // 4, _m % 4
    for _p in range(128):
        PERM[_m * 128 + _p] = _q * 512 + _p * 4 + _r


def alibi_slopes16():
    start = 2 ** (-(2 ** (-(math.log2(16) - 3))))
    return [float(start * start ** i) for i in range(16)]


SLOPES = alibi_slopes16()
KB_GBLK = [blocks_of(c)[b] for c in range(NCORES) for b in range(8)]


def hadamard(n):
    H = np.array([[1.0]], np.float32)
    while H.shape[0] < n:
        H = np.block([[H, H], [H, -H]]).astype(np.float32)
    return H


ENGS = ("sync", "gpsimd", "tensor", "vector", "scalar")


class Sem:
    __slots__ = ("h", "n", "name")

    def __init__(self, h, name):
        self.h = h
        self.n = 0
        self.name = name


class Prog:
    def __init__(self, nc, sems):
        self.nc = nc
        self.ops = {e: [] for e in ENGS}
        self.sems = sems
        self.lastw = {}
        self.esem = {"tensor": sems["T"], "vector": sems["V"],
                     "scalar": sems["S"], "gpsimd": sems["G"], "sync": sems["Y"]}
        self.dq = {}
        for q, pre in (("sync", "DY"), ("gpsimd", "DG"), ("scalar", "DS")):
            ring = [sems[pre + str(i)] for i in range(4)]
            self.dq[q] = {"ring": ring, "idx": 0, "cur": ring[0],
                          "prev": [None, None, None, None]}
        self.cc_last = None

    def emit(self, eng, fn, waits=(), inc=None):
        if eng in ("vector", "scalar"):
            s = self.esem[eng]
            if s.n > 0:
                waits = list(waits) + [(s, s.n)]
        if inc is True:
            s = self.esem[eng]
            s.n += 1
            incr = (s, s.n, 1)
        elif inc is not None:
            s, k = inc
            s.n += k
            incr = (s, s.n, k)
        else:
            incr = None
        ws = []
        for wv in waits:
            if wv is None:
                continue
            s, v = wv
            key = (eng, s.name)
            if self.lastw.get(key, -1) >= v:
                continue
            self.lastw[key] = v
            ws.append((s.h, v))
        inc_h = None if incr is None else (incr[0].h, incr[2])
        tok = None if incr is None else (incr[0], incr[1])

        def op(eng_obj, _fn=fn, _ws=tuple(ws), _inc=inc_h):
            for (h, v) in _ws:
                eng_obj.wait_ge(h, v)
            ins = _fn(eng_obj)
            if _inc is not None:
                ins.then_inc(_inc[0], _inc[1])
        self.ops[eng].append(op)
        return tok

    def dma(self, q, out, in_, waits=(), slow=False, new_group=True):
        st = self.dq[q]
        waits = [w for w in waits if w is not None]
        if new_group:
            st["idx"] = (st["idx"] + 1) % 4
            st["cur"] = st["ring"][st["idx"]]
            prev = st["prev"][st["idx"]]
            if prev is not None:
                waits = waits + [prev]
        sem = st["cur"]
        if slow:
            tok = self.emit(q, lambda e, o=out, i=in_:
                            e.dma_start(out=o, in_=i,
                                        allow_slow_non_contiguous=True),
                            waits=waits, inc=(sem, 16))
        else:
            tok = self.emit(q, lambda e, o=out, i=in_: e.dma_start(out=o, in_=i),
                            waits=waits, inc=(sem, 16))
        st["prev"][st["idx"]] = tok
        return tok

    def wait_only(self, eng, waits):
        ws = []
        for wv in waits:
            if wv is None:
                continue
            s, v = wv
            key = (eng, s.name)
            if self.lastw.get(key, -1) >= v:
                continue
            self.lastw[key] = v
            ws.append((s.h, v))
        if not ws:
            return

        def op(eng_obj, _ws=tuple(ws)):
            for (h, v) in _ws:
                eng_obj.wait_ge(h, v)
        self.ops[eng].append(op)

    def barrier(self):
        toks = [(s, s.n) for s in self.sems.values() if s.n > 0]
        for e in ENGS:
            self.wait_only(e, toks)


def build_nc():
    nc = bass.Bass("TRN2", target_bir_lowering=False)

    di = {}
    di["x"] = nc.dram_tensor("x", [T, D], F32, kind="ExternalInput")
    di["qpos"] = nc.dram_tensor("qpos", [T], F32, kind="ExternalInput")
    TSH = {"wq": (D, D), "wk": (D, D), "wv": (D, D), "wo": (D, D),
           "w1": (FH, D), "w3": (FH, D), "w2": (D, FH),
           "l1q": (D, D), "l1k": (512, D), "l1v": (512, D), "l1o": (D, D)}
    for name, (od, idim) in TSH.items():
        di[name + "_sh"] = nc.dram_tensor(name + "_sh", [od, idim // 8], F32,
                                          kind="ExternalInput")
    di["wg_sh"] = nc.dram_tensor("wg_sh", [D // 8, FH], F32, kind="ExternalInput")
    di["wu_sh"] = nc.dram_tensor("wu_sh", [D // 8, FH], F32, kind="ExternalInput")
    di["wd_sh"] = nc.dram_tensor("wd_sh", [FH // 8, D], F32, kind="ExternalInput")
    for bn in ("bq", "bk", "bv", "bo"):
        di[bn] = nc.dram_tensor(bn, [D], F32, kind="ExternalInput")
    y = nc.dram_tensor("y", [T, D], F32, kind="ExternalOutput")

    dint = {}

    def idram(name, shape, dt=BF16):
        dint[name] = nc.dram_tensor(name, shape, dt)
        return dint[name]

    for name, (od, idim) in TSH.items():
        idram(name + "_tsh", [idim // 8, od])
        if name not in ("w2",):
            idram(name + "_t", [idim, od])
    idram("w2_psh", [DCH, FH // 8, 128])
    idram("w2_pm", [DCH, FH, 128])
    idram("wd_psh", [DCH, FH // 8, 128])
    idram("wd_pm", [DCH, FH, 128])
    idram("wg_bsh", [D // 8, FH])
    idram("wg_b", [D, FH])
    idram("wu_bsh", [D // 8, FH])
    idram("wu_b", [D, FH])
    idram("wd_bsh", [FH // 8, D])
    idram("k0_sh", [D, T])
    idram("k0_ag", [NCORES * D, T])
    idram("v0_sh", [NH * T, DH])
    idram("v0_ag", [NCORES * NH * T, DH])
    idram("k1_sh", [512, T])
    idram("k1_ag", [NCORES * 512, T])
    idram("v1_sh", [KVH * T, DH])
    idram("v1_ag", [NCORES * KVH * T, DH])
    idram("nrm0_in", [1, 16], F32)
    idram("nrm0_out", [1, 16], F32)
    idram("nrm1_in", [1, 16], F32)
    idram("nrm1_out", [1, 16], F32)

    id_bf = nc.inline_tensor(np.eye(128, dtype=BFD), name="id_bf")
    id_f32c = nc.inline_tensor(np.eye(128, dtype=np.float32), name="id_f32c")
    h128_bf = nc.inline_tensor(hadamard(128).astype(BFD), name="h128_bf")
    ones_col_bfc = nc.inline_tensor(np.ones((128, 1), dtype=BFD), name="ones_col_bfc")
    ones_col_fc = nc.inline_tensor(np.ones((128, 1), dtype=np.float32), name="ones_col_fc")
    ones_row_fc = nc.inline_tensor(np.ones((1, 128), dtype=np.float32), name="ones_row_fc")
    negones_row_fc = nc.inline_tensor(np.full((1, 128), -1.0, dtype=np.float32),
                                      name="negones_row_fc")
    kpos_np = np.zeros((128, 64), np.float32)
    for kb in range(64):
        kpos_np[:, kb] = KB_GBLK[kb] * 128 + np.arange(128)
    kpos_c = nc.inline_tensor(kpos_np, name="kpos_c")
    cvals = sorted({-1.0, 0.0, 1.0, 2.0, -2.0, 0.5, 1.05, EPS, 1.0 / 2048.0, SCALE,
                    INV_SQRT_FH, BIGM, -BIGM} | {float(s) for s in SLOPES}
                   | {float(-s) for s in SLOPES})
    cv_fc = nc.inline_tensor(np.tile(np.asarray(cvals, np.float32)[None, :], (128, 1)),
                             name="cv_fc")
    cv_bc = nc.inline_tensor(np.tile(np.asarray(cvals, BFD)[None, :], (128, 1)),
                             name="cv_bc")

    ctxs = []
    sems = {}
    for name in ("T", "V", "S", "G", "Y", "CC0", "CC1", "CC2", "CC3",
                 "DY0", "DY1", "DY2", "DY3", "DG0", "DG1", "DG2", "DG3",
                 "DS0", "DS1", "DS2", "DS3"):
        g = nc.semaphore(name)
        sems[name] = Sem(g.__enter__(), name)
        ctxs.append(g)

    sbufs = {}

    def sbuf(name, shape, dt):
        g = nc.sbuf_tensor(name, shape, dt)
        t = g.__enter__()
        ctxs.append(g)
        sbufs[name] = t
        return t

    X = sbuf("xres", [128, DCH * T], F32)
    A = sbuf("arA", [128, DCH * T], BF16)
    B = sbuf("arB", [128, 2 * DCH * T], BF16)
    W = sbuf("arW", [128, 8192], BF16)
    SCR = sbuf("scr", [128, 4096], F32)
    MISC = sbuf("misc", [128, 640], F32)
    CONS = sbuf("cons", [128, 600], F32)
    CONB = sbuf("conb", [128, 512], BF16)
    ROWS = sbuf("rows", [1, 2048], F32)
    QH = sbuf("qh", [128, 1024], BF16)

    g = nc.psum_tensor("psA", [128, 1024], F32)
    psA = g.__enter__()
    ctxs.append(g)
    g = nc.psum_tensor("psB", [128, 1024], F32)
    psB = g.__enter__()
    ctxs.append(g)
    g = nc.psum_tensor("psC", [128, 512], F32)
    psC = g.__enter__()
    ctxs.append(g)
    g = nc.psum_tensor("psD", [128, 512], F32)
    psD = g.__enter__()
    ctxs.append(g)
    g = nc.psum_tensor("psT", [128, 1024], BF16)
    psT = g.__enter__()
    ctxs.append(g)
    g = nc.psum_tensor("ps7", [128, 512], F32)
    ps7 = g.__enter__()
    ctxs.append(g)

    class PsView:
        def __init__(self, h, off):
            self.h = h
            self.off = off

        def __getitem__(self, idx):
            pi, fi = idx
            return self.h[pi, self.off + (fi.start or 0):
                          self.off + (fi.stop if fi.stop is not None else 512)]

    ps = [PsView(psA, 0), PsView(psA, 512), PsView(psB, 0), PsView(psB, 512),
          PsView(psC, 0), PsView(psD, 0)]

    p = Prog(nc, sems)
    BP = 2 * DCH * T
    AP_ = bass.AP

    C_ID, C_ONE, C_KPOS, C_CV, C_BIAS, C_OROW, C_NROW = 0, 128, 129, 200, 260, 330, 458
    B_ID, B_H128, B_ONE, B_CV = 0, 128, 256, 260
    for i, v in enumerate(cvals):
        nc.const_aps.register(F32, v, CONS[:, C_CV + i:C_CV + i + 1])
        nc.const_aps.register(BF16, v, CONB[:, B_CV + i:B_CV + i + 1])
    R_MEAN, R_INV = 0, 1024
    R_KNL, R_KMG, R_QM, R_PRD, R_SQ, R_CP, R_CN = 3100, 3116, 3132, 3148, 3164, 3180, 3196
    R_RCP = 3212
    R_PACK = 3500
    ONES_ROW = CONS[0:1, C_OROW:C_OROW + 128]
    NEG_ROW = CONS[0:1, C_NROW:C_NROW + 128]
    ONE_COL_F = CONS[:, C_ONE:C_ONE + 1]
    ONE_COL_B = CONB[:, B_ONE:B_ONE + 1]

    p.dma("sync", CONS[:, C_ID:C_ID + 128], id_f32c[:, :])
    p.dma("sync", CONS[:, C_ONE:C_ONE + 1], ones_col_fc[:, :], new_group=False)
    p.dma("sync", CONS[:, C_KPOS:C_KPOS + 64], kpos_c[:, :], new_group=False)
    p.dma("sync", CONS[:, C_CV:C_CV + len(cvals)], cv_fc[:, :], new_group=False)
    p.dma("sync", CONS[0:1, C_OROW:C_OROW + 128], ones_row_fc[0:1, :], new_group=False)
    p.dma("sync", CONS[0:1, C_NROW:C_NROW + 128], negones_row_fc[0:1, :], new_group=False)
    p.dma("sync", CONB[:, B_ID:B_ID + 128], id_bf[:, :], new_group=False)
    p.dma("sync", CONB[:, B_H128:B_H128 + 128], h128_bf[:, :], new_group=False)
    p.dma("sync", CONB[:, B_ONE:B_ONE + 1], ones_col_bfc[:, :], new_group=False)
    t_consts = p.dma("sync", CONB[:, B_CV:B_CV + len(cvals)], cv_bc[:, :],
                     new_group=False)
    for i, bn in enumerate(("bq", "bk", "bv", "bo")):
        t_consts = p.dma("sync", CONS[:, C_BIAS + 16 * i:C_BIAS + 16 * (i + 1)],
                         AP_(di[bn], 0, [[1, 128], [128, 16]]), slow=True,
                         new_group=False)

    ptok = {}

    def psw(bank):
        t = ptok.get(bank)
        return [t] if t is not None else []

    def mm(bank_ap, lhsT, rhs, start, stop, waits=(), inc=None):
        return p.emit("tensor",
                      lambda e, o=bank_ap, l=lhsT, r=rhs, st=start, sp=stop:
                      e.matmul(o, l, r, start=st, stop=sp),
                      waits=waits, inc=inc)

    def ev(fn, waits=(), inc=True, eng="vector"):
        return p.emit(eng, fn, waits=waits, inc=inc)

    scr_cons = [None, None]
    for tch in range(8):
        half = tch % 2
        base = half * 2048
        tdma = p.dma("sync", SCR[:, base:base + 2048],
                     di["x"][tch * 128:(tch + 1) * 128, :],
                     waits=[scr_cons[half], t_consts])
        last_cp = None
        for grp in range(4):
            bank = grp % 2
            tmm = None
            for j in range(4):
                dcb = grp * 4 + j
                tmm = p.emit(
                    "tensor",
                    lambda e, o=ps[bank][:, j * 128:(j + 1) * 128],
                    i_=SCR[:, base + dcb * 128:base + (dcb + 1) * 128],
                    idn=CONS[:, C_ID:C_ID + 128]: e.transpose(o, i_, idn),
                    waits=([tdma] + psw(bank)) if j == 0 else [],
                    inc=True if j == 3 else None)
            tcp = ev(lambda e, o=AP_(X.ap().tensor, (grp * 4) * T + tch * 128,
                                     [[DCH * T, 128], [T, 4], [1, 128]]),
                     i_=ps[bank][:, 0:512], _w=None:
                     e.tensor_copy(o, i_), waits=[tmm])
            ptok[bank] = tcp
            last_cp = tcp
        scr_cons[half] = last_cp

    prep_acons = [None]

    def prep_transpose(name, od, isz):
        ish = di[name + "_sh"]
        osh = dint[name + "_tsh"]
        nob = od // 128
        toks = []
        per = max(1, 2048 // isz * 4)
        ofs = 0
        while ofs < nob:
            cnt = min(per, nob - ofs)
            tk = p.dma("gpsimd", B[:, ofs * isz:(ofs + cnt) * isz],
                       AP_(ish, ofs * 128 * isz,
                           [[isz, 128], [128 * isz, cnt], [1, isz]]),
                       new_group=(ofs == 0))
            toks.append(tk)
            ofs += cnt
        tld = toks[-1]
        tst = None
        a_cons = prep_acons[0]
        for icb in range(isz // 128):
            for ob4 in range((nob + 3) // 4):
                jn = min(4, nob - ob4 * 4)
                tmm = None
                for j in range(jn):
                    ocb = ob4 * 4 + j
                    tmm = p.emit(
                        "tensor",
                        lambda e, o=psT[:, j * 128:(j + 1) * 128],
                        i_=B[:, ocb * isz + icb * 128:ocb * isz + icb * 128 + 128],
                        idn=CONB[:, B_ID:B_ID + 128]: e.transpose(o, i_, idn),
                        waits=([tld, t_consts, a_cons] + psw("T")) if j == 0 else [],
                        inc=True if j == jn - 1 else None)
                tcp = ev(lambda e, o=AP_(B.ap().tensor, 16384 + ob4 * 4 * 128,
                                         [[BP, 128], [1, jn * 128]]),
                         i_=psT[:, 0:jn * 128]: e.tensor_copy(o, i_),
                         waits=[tmm])
                ptok["T"] = tcp
            tst = p.dma("gpsimd", osh[icb * 128:(icb + 1) * 128, :],
                        B[:, 16384:16384 + od], waits=[ptok["T"]])
            a_cons = tst
        prep_acons[0] = tst
        return tst

    cc_ring = {"idx": 0, "prev": [None, None, None, None]}

    def do_cc(kind, op, in_ap, out_ap, waits):
        cc_ring["idx"] = (cc_ring["idx"] + 1) % 4
        idx = cc_ring["idx"]
        sem = sems["CC" + str(idx)]
        w = list(waits) + ([cc_ring["prev"][idx]] if cc_ring["prev"][idx] else [])
        tok = p.emit("gpsimd",
                     lambda e, i=in_ap, o=out_ap, k=kind, o_=op:
                     e.collective_compute(k, o_,
                                          replica_groups=[list(range(NCORES))],
                                          ins=[i], outs=[o]),
                     waits=w, inc=(sem, 1))
        cc_ring["prev"][idx] = tok
        return tok

    def do_ag(in_ap, out_ap, waits):
        return do_cc("AllGather", AO.bypass, in_ap, out_ap, waits)

    ag = {}

    prep_st = {}

    def prep_one(name, ag_now=True):
        od, idim = TSH[name]
        tst = prep_transpose(name, od, idim // 8)
        prep_st[name] = tst
        if not ag_now:
            return
        prep_ag(name)

    def prep_ag(name):
        od, idim = TSH[name]
        tst = prep_st[name]
        if name == "w2":
            lastc = None
            for oc in range(DCH):
                lastc = p.dma("gpsimd",
                              AP_(dint["w2_psh"], oc * (FH // 8) * 128,
                                  [[128, FH // 8], [1, 128]]),
                              AP_(dint["w2_tsh"], oc * 128,
                                  [[D, FH // 8], [1, 128]]),
                              waits=[tst] if oc == 0 else [],
                              new_group=(oc == 0))
            agl = None
            for oc in range(DCH):
                agl = do_ag(AP_(dint["w2_psh"], oc * (FH // 8) * 128,
                                [[128, FH // 8], [1, 128]]).opt(),
                            AP_(dint["w2_pm"], oc * FH * 128,
                                [[128, FH], [1, 128]]).opt(),
                            [lastc])
            ag["w2"] = agl
        else:
            ag[name] = do_ag(dint[name + "_tsh"].ap().opt(),
                             dint[name + "_t"].ap().opt(), [tst])

    def prep_cast(src, dst, rows, cols):
        sts = []
        for i, r0 in enumerate(range(0, rows, 128)):
            half = i % 2
            base = half * cols
            if cols > 8192:
                raise ValueError("prep_cast cols too big")
            tk = p.dma("gpsimd", B[:, base:base + cols], src[r0:r0 + 128, :],
                       waits=[sts[-2]] if len(sts) >= 2 else [])
            tst = p.dma("gpsimd", dst[r0:r0 + 128, :], B[:, base:base + cols],
                        waits=[tk])
            sts.append(tst)
        return sts

    def prep_rest():
        for name in ("w1", "w3", "w2", "wq", "wo", "l1q", "l1k", "l1v", "l1o"):
            prep_one(name, ag_now=False)
        for name in ("w1", "w3", "w2", "wq", "wo", "l1q", "l1k", "l1v", "l1o"):
            prep_ag(name)
        tg = prep_cast(di["wg_sh"], dint["wg_bsh"], D // 8, FH)
        ag["wg"] = do_ag(dint["wg_bsh"].ap().opt(), dint["wg_b"].ap().opt(), tg)
        tu = prep_cast(di["wu_sh"], dint["wu_bsh"], D // 8, FH)
        ag["wu"] = do_ag(dint["wu_bsh"].ap().opt(), dint["wu_b"].ap().opt(), tu)
        td = prep_cast(di["wd_sh"], dint["wd_bsh"], FH // 8, D)
        lastc = None
        for oc in range(DCH):
            lastc = p.dma("gpsimd",
                          AP_(dint["wd_psh"], oc * (FH // 8) * 128,
                              [[128, FH // 8], [1, 128]]),
                          AP_(dint["wd_bsh"], oc * 128, [[D, FH // 8], [1, 128]]),
                          waits=td if oc == 0 else [],
                          new_group=(oc == 0))
        agl = None
        for oc in range(DCH):
            agl = do_ag(AP_(dint["wd_psh"], oc * (FH // 8) * 128,
                            [[128, FH // 8], [1, 128]]).opt(),
                        AP_(dint["wd_pm"], oc * FH * 128,
                            [[128, FH], [1, 128]]).opt(),
                        [lastc])
        ag["wd"] = agl

    def emit_proj(wt, idim, od, rhs_buf, rhs_off, rhs_stride, tblocks, writer,
                  pan_buf, pan_off, ag_tok, pan_waits=()):
        nich = idim // 128
        npan = od // 256
        bank_i = 0
        pan_cons = [None, None]
        first = True
        for pan in range(npan):
            half = pan % 2
            pofs = pan_off + half * nich * 256
            w0 = [pan_cons[half], ag_tok] + (list(pan_waits) if first else [])
            first = False
            tks_ = []
            for qi, seg in enumerate(range(0, nich, 8)):
                cnt = min(8, nich - seg)
                tks_.append(p.dma("sync" if qi % 2 == 0 else "scalar",
                                  pan_buf[:, pofs + seg * 256:
                                          pofs + (seg + cnt) * 256],
                                  AP_(wt, pan * 256 + seg * 128 * od,
                                      [[od, 128], [128 * od, cnt], [1, 256]]),
                                  waits=w0))
            tk = tks_
            last_mm = None
            for occ in range(2):
                oc = pan * 2 + occ
                for (tb, t0, tw) in tblocks:
                    bank = bank_i % 4
                    bank_i += 1
                    tmm = None
                    for ic in range(nich):
                        tmm = mm(ps[bank][:, 0:tw],
                                 pan_buf[:, pofs + ic * 256 + occ * 128:
                                         pofs + ic * 256 + occ * 128 + 128],
                                 rhs_buf[:, rhs_off + ic * rhs_stride + t0:
                                         rhs_off + ic * rhs_stride + t0 + tw],
                                 start=(ic == 0), stop=(ic == nich - 1),
                                 waits=(tk + psw(bank)) if ic == 0 else [],
                                 inc=True if ic == nich - 1 else None)
                    ctok = writer(oc, tb, t0, tw, ps[bank][:, 0:tw], tmm)
                    ptok[bank] = ctok
                    last_mm = tmm
            pan_cons[half] = last_mm

    LS = {}

    def phase_ln(L):
        prev_tsr = None
        for tb in range(2):
            t0 = tb * 512
            tmm1 = None
            for ic in range(DCH):
                tmm1 = mm(ps[2][0:1, 0:512], ONE_COL_F,
                          X[:, ic * T + t0: ic * T + t0 + 512],
                          start=(ic == 0), stop=(ic == DCH - 1),
                          waits=([t_consts] + psw(2)) if ic == 0 else [],
                          inc=True if ic == DCH - 1 else None)
            sq_cons = [None, None]
            tmm2 = None
            for ic in range(DCH):
                half = ic % 2
                sq = SCR[:, half * 512:(half + 1) * 512]
                tsq = ev(lambda e, o=sq, a=X[:, ic * T + t0: ic * T + t0 + 512]:
                         e.tensor_tensor(o, a, a, AO.mult),
                         waits=[sq_cons[half]])
                tmm2 = mm(ps[3][0:1, 0:512], ONE_COL_F, sq,
                          start=(ic == 0), stop=(ic == DCH - 1),
                          waits=[tsq] + (psw(3) if ic == 0 else []),
                          inc=True)
                sq_cons[half] = tmm2
            tme = ev(lambda e, o=ROWS[0:1, R_MEAN + t0:R_MEAN + t0 + 512],
                     i_=ps[2][0:1, 0:512]:
                     e.tensor_single_scalar(o, i_, 1.0 / 2048.0, AO.mult),
                     waits=[tmm1])
            ptok[2] = tme
            tv1 = ev(lambda e, i_=ps[3][0:1, 0:512]:
                     e.tensor_single_scalar(SCR[0:1, 2560:3072], i_, 1.0 / 2048.0,
                                            AO.mult), waits=[tmm2])
            ptok[3] = tv1
            tv2 = ev(lambda e, m=ROWS[0:1, R_MEAN + t0:R_MEAN + t0 + 512]:
                     e.tensor_tensor(SCR[0:1, 2048:2560], m, m, AO.mult),
                     waits=[prev_tsr] if prev_tsr is not None else [])
            tv3 = ev(lambda e: e.tensor_tensor(SCR[0:1, 2560:3072],
                                               SCR[0:1, 2560:3072],
                                               SCR[0:1, 2048:2560], AO.subtract),
                     waits=[])
            tv4 = ev(lambda e: e.tensor_single_scalar(SCR[0:1, 2560:3072],
                                                      SCR[0:1, 2560:3072], EPS,
                                                      AO.add), waits=[])
            tr = ev(lambda e: e.reciprocal(SCR[0:1, 2048:2560], SCR[0:1, 2560:3072]),
                    waits=[])
            tsr = ev(lambda e, o=ROWS[0:1, R_INV + t0:R_INV + t0 + 512]:
                     e.activation(o, SCR[0:1, 2048:2560], AF.Sqrt),
                     waits=[tr], eng="scalar")
            LS[("lnst", tb)] = (tme, tsr)
            prev_tsr = tsr
        for tb in range(2):
            t0 = tb * 512
            tme, tsr = LS[("lnst", tb)]
            tbm = mm(ps[2 + tb][:, 0:512], ONES_ROW,
                     ROWS[0:1, R_MEAN + t0:R_MEAN + t0 + 512],
                     start=True, stop=True, waits=[tme] + psw(2 + tb), inc=True)
            tbi = mm(ps[4 + tb][:, 0:512], ONES_ROW,
                     ROWS[0:1, R_INV + t0:R_INV + t0 + 512],
                     start=True, stop=True, waits=[tsr] + psw(4 + tb), inc=True)
            LS[("lnb", tb)] = (tbm, tbi)
        last = None
        for ic in range(DCH):
            for tb in range(2):
                t0 = tb * 512
                tbm, tbi = LS[("lnb", tb)]
                half = ic % 2
                tmp = SCR[:, half * 512:(half + 1) * 512]
                ev(lambda e, o=tmp, a=X[:, ic * T + t0: ic * T + t0 + 512],
                   b=ps[2 + tb][:, 0:512]:
                   e.tensor_tensor(o, a, b, AO.subtract), waits=[tbm, tbi])
                last = ev(lambda e, o=A[:, ic * T + t0: ic * T + t0 + 512],
                          a=tmp, b=ps[4 + tb][:, 0:512]:
                          e.tensor_tensor(o, a, b, AO.mult), waits=[])
        for b in (2, 3, 4, 5):
            ptok[b] = last
        LS["xn"] = last

    def phase_bvb():
        tld = p.dma("sync", ROWS[0:1, 0:2048], AP_(di["bv"], 0, [[2048, 1], [1, 2048]]))
        toks = []
        for i4 in range(4):
            bank = i4 % 2
            tb1 = mm(ps[bank][:, 0:512], ONES_ROW,
                     ROWS[0:1, i4 * 512:(i4 + 1) * 512],
                     start=True, stop=True,
                     waits=[tld, t_consts] + psw(bank), inc=True)
            tc = ev(lambda e, o=SCR[:, i4 * 512:(i4 + 1) * 512],
                    i_=ps[bank][:, 0:512]: e.tensor_copy(o, i_), waits=[tb1])
            ptok[bank] = tc
            toks.append(tc)
        LS["bvb"] = toks[-1]

    def phase_kv(L):
        xn_tok = LS["xn"]
        kvh = NH if L == 0 else KVH
        kd = kvh * DH
        wkt = dint["wk_t"] if L == 0 else dint["l1k_t"]
        wvt = dint["wv_t"] if L == 0 else dint["l1v_t"]
        ksh = dint["k0_sh"] if L == 0 else dint["k1_sh"]
        vsh = dint["v0_sh"] if L == 0 else dint["v1_sh"]
        bias_k = (C_BIAS + 16) if L == 0 else None
        st_toks = []
        nprev = [None]

        def wr_k(oc, tb, t0, tw, psa, tmm):
            stg = B[:, 24576 + (oc % 2) * 1024 + t0:24576 + (oc % 2) * 1024 + t0 + tw]
            w = [tmm] + ([st_toks[-4]] if len(st_toks) >= 4 else [])
            if bias_k is not None:
                tkk = ev(lambda e, o=stg, i_=psa,
                         bc=CONS[:, bias_k + oc:bias_k + oc + 1]:
                         e.tensor_scalar(o, i_, bc, None, AO.add), waits=w)
            else:
                tkk = ev(lambda e, o=stg, i_=psa: e.tensor_copy(o, i_), waits=w)
            tsq = ev(lambda e, o=QH[:, 0:tw], a=stg:
                     e.tensor_tensor(o, a, a, AO.mult),
                     waits=[tkk] + ([nprev[0]] if nprev[0] else []))
            tnm = mm(ps7[0:1, 0:tw], ONE_COL_B, QH[:, 0:tw],
                     start=True, stop=True, waits=[tsq] + psw(7), inc=True)
            nprev[0] = tnm
            trd = ev(lambda e, o=MISC[0:1, 608 + oc:609 + oc], i_=ps7[0:1, 0:tw]:
                     e.tensor_reduce(o, i_, AX.X, AO.max), waits=[tnm])
            ptok[7] = trd
            if tb == 0:
                ev(lambda e, oc_=oc:
                   e.tensor_copy(SCR[0:1, R_KNL + oc_:R_KNL + oc_ + 1],
                                 MISC[0:1, 608 + oc_:609 + oc_]), waits=[])
            else:
                ev(lambda e, oc_=oc:
                   e.tensor_tensor(SCR[0:1, R_KNL + oc_:R_KNL + oc_ + 1],
                                   MISC[0:1, 608 + oc_:609 + oc_],
                                   SCR[0:1, R_KNL + oc_:R_KNL + oc_ + 1],
                                   AO.max), waits=[])
            tstd = p.dma("gpsimd", ksh[oc * 128:(oc + 1) * 128, t0:t0 + tw], stg,
                         waits=[tkk])
            st_toks.append(tstd)
            return tkk

        emit_proj(wkt, D, kd, A, 0, T, [(0, 0, 512), (1, 512, 512)], wr_k,
                  W, 0, ag["wk" if L == 0 else "l1k"], pan_waits=[xn_tok])
        kag_t = do_ag(ksh.ap().opt(),
                      (dint["k0_ag"] if L == 0 else dint["k1_ag"]).ap().opt(),
                      st_toks)
        tz = ev(lambda e: e.memset(SCR[0:1, R_PACK:R_PACK + 16], 0.0), waits=[])
        tpk = ev(lambda e, k=kvh: e.tensor_copy(MISC[0:1, 624:624 + k],
                                                SCR[0:1, R_KNL:R_KNL + k]),
                 waits=[tz])
        nin = dint["nrm0_in"] if L == 0 else dint["nrm1_in"]
        nout = dint["nrm0_out"] if L == 0 else dint["nrm1_out"]
        tnd = p.dma("gpsimd", nin.ap(), SCR[0:1, R_PACK:R_PACK + 16], waits=[tpk])
        t_ar = do_cc("AllReduce", AO.max, nin.ap().opt(), nout.ap().opt(), [tnd])
        tldk = p.dma("sync", SCR[0:1, R_KMG:R_KMG + 16], nout.ap(), waits=[t_ar])
        LS["knorm"] = tldk

        p.barrier()
        vst = []
        pan_cons = [None, None]
        for pan in range(kd // 256):
            half = pan % 2
            pofs = half * DCH * 256
            tk = None
            for seg in range(0, DCH, 8):
                tk = p.dma("sync", W[:, pofs + seg * 256: pofs + (seg + 8) * 256],
                           AP_(wvt, pan * 256 + seg * 128 * kd,
                               [[kd, 128], [128 * kd, 8], [1, 256]]),
                           waits=[pan_cons[half], ag["wv" if L == 0 else "l1v"]]
                           if seg == 0 else [],
                           new_group=(seg == 0))
            last_mm = None
            for tch in range(8):
                bank = tch % 4
                tmm = None
                for ic in range(DCH):
                    tmm = mm(ps[bank][:, 0:256],
                             A[:, ic * T + tch * 128: ic * T + (tch + 1) * 128],
                             W[:, pofs + ic * 256:(pofs + (ic + 1) * 256)],
                             start=(ic == 0), stop=(ic == DCH - 1),
                             waits=([tk, xn_tok] + psw(bank)) if ic == 0 else [],
                             inc=True if ic == DCH - 1 else None)
                stg = B[:, 26624 + (tch % 4) * 512: 26624 + (tch % 4) * 512 + 256]
                wv_w = [tmm] + (vst[-8:-6] if len(vst) >= 8 else [])
                if L == 0:
                    tkv = ev(lambda e, o=stg, i_=ps[bank][:, 0:256],
                             b=SCR[:, pan * 256:(pan + 1) * 256]:
                             e.tensor_tensor(o, i_, b, AO.add),
                             waits=wv_w + [LS["bvb"]])
                else:
                    tkv = ev(lambda e, o=stg, i_=ps[bank][:, 0:256]:
                             e.tensor_copy(o, i_), waits=wv_w)
                ptok[bank] = tkv
                last_mm = tmm
                for hh in range(2):
                    h = pan * 2 + hh
                    tstd = p.dma("gpsimd",
                                 vsh[h * T + tch * 128: h * T + (tch + 1) * 128, :],
                                 stg[:, hh * 128:(hh + 1) * 128], waits=[tkv])
                    vst.append(tstd)
            pan_cons[half] = last_mm
        vag_t = do_ag(vsh.ap().opt(),
                      (dint["v0_ag"] if L == 0 else dint["v1_ag"]).ap().opt(),
                      vst)
        LS["kv_ag"] = (kag_t, vag_t)

    def sparsify_group(ch0, tprev, hid_waits=()):
        def h_(r):
            return SCR[:, r * 256:(r + 1) * 256]

        def s_(k):
            if k < 4:
                return SCR[:, 2048 + k * 256:2048 + (k + 1) * 256]
            return SCR[:, 1024 + (k - 4) * 256:1024 + (k - 3) * 256]
        ev(lambda e: e.tensor_tensor(s_(0), h_(0), h_(0), AO.mult), waits=[tprev])
        ev(lambda e: e.tensor_tensor(s_(1), h_(1), h_(1), AO.mult), waits=[])
        ev(lambda e: e.tensor_tensor(s_(2), h_(2), h_(2), AO.mult), waits=[])
        ev(lambda e: e.tensor_tensor(s_(3), h_(3), h_(3), AO.mult), waits=[])
        ev(lambda e: e.tensor_tensor(s_(4), s_(0), s_(1), AO.min), waits=[])
        ev(lambda e: e.tensor_tensor(s_(5), s_(2), s_(3), AO.min), waits=[])
        ev(lambda e: e.tensor_tensor(s_(0), s_(0), s_(1), AO.max), waits=[])
        ev(lambda e: e.tensor_tensor(s_(2), s_(2), s_(3), AO.max), waits=[])
        ev(lambda e: e.tensor_tensor(s_(0), s_(0), s_(2), AO.min), waits=[])
        ev(lambda e: e.tensor_tensor(s_(4), s_(4), s_(5), AO.max), waits=[])
        ev(lambda e: e.tensor_tensor(s_(0), s_(0), s_(4), AO.max), waits=[])
        outs = None
        for r in range(4):
            ev(lambda e, r_=r: e.tensor_tensor(s_(1), h_(r_), h_(r_), AO.mult),
               waits=[])
            ev(lambda e: e.tensor_tensor(s_(1), s_(1), s_(0), AO.is_ge), waits=[])
            outs = ev(lambda e, r_=r, c0=ch0:
                      e.tensor_tensor(B[:, (c0 + r_) * TBF:(c0 + r_ + 1) * TBF],
                                      h_(r_), s_(1), AO.mult),
                      waits=hid_waits)
        return outs

    def fwht_stages(tprev):
        last = tprev
        for ch2 in range(FCH // 2):
            bank = ch2 % 2
            tmm = None
            for j in range(2):
                ch = ch2 * 2 + j
                tmm = mm(ps[bank][:, j * 256:(j + 1) * 256],
                         CONB[:, B_H128:B_H128 + 128],
                         B[:, ch * TBF:(ch + 1) * TBF],
                         start=True, stop=True,
                         waits=([tprev, t_consts] + psw(bank)) if j == 0 else [],
                         inc=True if j == 1 else None)
            tcp = ev(lambda e, o=B[:, ch2 * 512:(ch2 + 1) * 512],
                     i_=ps[bank][:, 0:512]: e.tensor_copy(o, i_), waits=[tmm])
            ptok[bank] = tcp
            last = tcp
        srcb, dstb = 0, FCH * TBF
        h = 1
        while h < 64:
            nb = 64 // (2 * h)
            ev(lambda e,
               o=AP_(B.ap().tensor, dstb, [[BP, 128], [2 * h * TBF, nb], [1, h * TBF]]),
               a=AP_(B.ap().tensor, srcb, [[BP, 128], [2 * h * TBF, nb], [1, h * TBF]]),
               b=AP_(B.ap().tensor, srcb + h * TBF,
                     [[BP, 128], [2 * h * TBF, nb], [1, h * TBF]]):
               e.tensor_tensor(o, a, b, AO.add), waits=[last])
            last = ev(lambda e,
                      o=AP_(B.ap().tensor, dstb + h * TBF,
                            [[BP, 128], [2 * h * TBF, nb], [1, h * TBF]]),
                      a=AP_(B.ap().tensor, srcb,
                            [[BP, 128], [2 * h * TBF, nb], [1, h * TBF]]),
                      b=AP_(B.ap().tensor, srcb + h * TBF,
                            [[BP, 128], [2 * h * TBF, nb], [1, h * TBF]]):
                      e.tensor_tensor(o, a, b, AO.subtract), waits=[])
            srcb, dstb = dstb, srcb
            h *= 2
        return last

    def phase_ffn(L):
        xn_tok = LS["xn"]
        w_g = dint["w1_t"] if L == 0 else dint["wg_b"]
        w_u = dint["w3_t"] if L == 0 else dint["wu_b"]
        wpan = dint["w2_pm"] if L == 0 else dint["wd_pm"]
        agg = ag["w1" if L == 0 else "wg"]
        agu = ag["w3" if L == 0 else "wu"]
        agd = ag["w2" if L == 0 else "wd"]
        PANR = FCH * TBF
        down_cons = [None, None]
        pan_cons_W = [None, None]
        for tb in range(NTBF):
            t0 = tb * TBF
            pan_cons_B = [down_cons[0], down_cons[1]]
            dprev = [t for t in down_cons if t is not None]
            hid_last = None
            l1last = None
            last_th = None
            for pan in range(FH // 256):
                half = pan % 2
                pofs = PANR + half * DCH * 256
                uofs = half * DCH * 256
                tkg1 = p.dma("sync",
                             B[:, pofs: pofs + 8 * 256],
                             AP_(w_g, pan * 256,
                                 [[FH, 128], [128 * FH, 8], [1, 256]]),
                             waits=[pan_cons_B[half], agg])
                tkg2 = p.dma("scalar",
                             B[:, pofs + 8 * 256: pofs + 16 * 256],
                             AP_(w_g, pan * 256 + 8 * 128 * FH,
                                 [[FH, 128], [128 * FH, 8], [1, 256]]),
                             waits=[pan_cons_B[half], agg])
                tku1 = p.dma("gpsimd",
                             W[:, uofs: uofs + 8 * 256],
                             AP_(w_u, pan * 256,
                                 [[FH, 128], [128 * FH, 8], [1, 256]]),
                             waits=[pan_cons_W[half], agu])
                tku2 = p.dma("sync",
                             W[:, uofs + 8 * 256: uofs + 16 * 256],
                             AP_(w_u, pan * 256 + 8 * 128 * FH,
                                 [[FH, 128], [128 * FH, 8], [1, 256]]),
                             waits=[pan_cons_W[half], agu])
                tkg = [tkg1, tkg2]
                tku = [tku1, tku2]
                for occ in range(2):
                    ch = pan * 2 + occ
                    bg = (ch % 3) * 2
                    bu = (ch % 3) * 2 + 1
                    tgm = tum = None
                    for ic in range(DCH):
                        tgm = mm(ps[bg][:, 0:TBF],
                                 B[:, pofs + ic * 256 + occ * 128:
                                   pofs + ic * 256 + occ * 128 + 128],
                                 A[:, ic * T + t0: ic * T + t0 + TBF],
                                 start=(ic == 0), stop=(ic == DCH - 1),
                                 waits=(tkg + [xn_tok] + psw(bg)) if ic == 0 else [],
                                 inc=True if ic == DCH - 1 else None)
                    for ic in range(DCH):
                        tum = mm(ps[bu][:, 0:TBF],
                                 W[:, uofs + ic * 256 + occ * 128:
                                   uofs + ic * 256 + occ * 128 + 128],
                                 A[:, ic * T + t0: ic * T + t0 + TBF],
                                 start=(ic == 0), stop=(ic == DCH - 1),
                                 waits=(tku + psw(bu)) if ic == 0 else [],
                                 inc=True if ic == DCH - 1 else None)
                    pan_cons_B[half] = tgm
                    pan_cons_W[half] = tum
                    tsl = ev(lambda e, o=QH[:, 0:TBF], i_=ps[bg][:, 0:TBF]:
                             e.activation(o, i_, AF.Silu),
                             waits=[tgm] + ([last_th] if last_th else []),
                             eng="scalar")
                    ptok[bg] = tsl
                    if L == 0:
                        th = ev(lambda e, o=B[:, ch * TBF:(ch + 1) * TBF],
                                u_=ps[bu][:, 0:TBF], g_=QH[:, 0:TBF]:
                                e.scalar_tensor_tensor(o, u_, INV_SQRT_FH, g_,
                                                       AO.mult, AO.mult),
                                waits=[tsl, tum] + dprev)
                        ptok[bu] = th
                        hid_last = th
                        last_th = th
                    else:
                        gi = ch % 4
                        th = ev(lambda e, o=SCR[:, gi * 256:(gi + 1) * 256],
                                u_=ps[bu][:, 0:TBF], g_=QH[:, 0:TBF]:
                                e.scalar_tensor_tensor(o, u_, 1.0, g_,
                                                       AO.mult, AO.mult),
                                waits=[tsl, tum])
                        ptok[bu] = th
                        l1last = th
                        last_th = th
                        if gi == 3:
                            hid_last = sparsify_group(ch - 3, l1last,
                                                      hid_waits=dprev)
            if L == 0:
                ht = fwht_stages(hid_last)
            else:
                ht = hid_last
            for oc in range(DCH):
                half = oc % 2
                dofs = PANR + half * FCH * 128
                tk = p.dma("sync", B[:, dofs:dofs + FCH * 128],
                           AP_(wpan, oc * FH * 128,
                               [[128, 128], [128 * 128, FCH], [1, 128]]),
                           waits=[down_cons[half], ht, agd])
                bank = 4 + half
                tmm = None
                for ic in range(FCH):
                    tmm = mm(ps[bank][:, 0:TBF],
                             B[:, dofs + ic * 128: dofs + (ic + 1) * 128],
                             B[:, ic * TBF:(ic + 1) * TBF],
                             start=(ic == 0), stop=(ic == FCH - 1),
                             waits=([tk, ht] + psw(bank)) if ic == 0 else [],
                             inc=True if ic == FCH - 1 else None)
                tadd = ev(lambda e, o=X[:, oc * T + t0: oc * T + t0 + TBF],
                          i_=ps[bank][:, 0:TBF]:
                          e.tensor_tensor(o, i_, o, AO.add), waits=[tmm])
                ptok[bank] = tadd
                down_cons[half] = tmm
        p.barrier()

    def phase_q(L):
        xn_tok = LS["xn"]
        wqt = dint["wq_t"] if L == 0 else dint["l1q_t"]
        bias_q = C_BIAS if L == 0 else None
        tqp = p.dma("sync", ROWS[0:1, R_INV:R_INV + T],
                    AP_(di["qpos"], 0, [[T, 1], [1, T]]))
        LS["qpos"] = tqp
        nprev = [None]

        def wr_q(oc, tb, t0, tw, psa, tmm):
            dst = B[:, oc * T + t0: oc * T + t0 + tw]
            if bias_q is not None:
                t1 = ev(lambda e, o=dst, i_=psa,
                        bc=CONS[:, bias_q + oc:bias_q + oc + 1]:
                        e.tensor_scalar(o, i_, bc, SCALE, AO.add, AO.mult),
                        waits=[tmm])
            else:
                t1 = ev(lambda e, o=dst, i_=psa:
                        e.tensor_single_scalar(o, i_, SCALE, AO.mult), waits=[tmm])
            tsq = ev(lambda e, o=QH[:, 0:tw], a=dst:
                     e.tensor_tensor(o, a, a, AO.mult),
                     waits=[t1] + ([nprev[0]] if nprev[0] else []))
            tnm = mm(ps7[0:1, 0:tw], ONE_COL_B, QH[:, 0:tw],
                     start=True, stop=True, waits=[tsq] + psw(7), inc=True)
            nprev[0] = tnm
            trd = ev(lambda e, o=MISC[0:1, 608 + oc:609 + oc], i_=ps7[0:1, 0:tw]:
                     e.tensor_reduce(o, i_, AX.X, AO.max), waits=[tnm])
            ptok[7] = trd
            if tb == 0:
                ev(lambda e, oc_=oc:
                   e.tensor_copy(SCR[0:1, R_QM + oc_:R_QM + oc_ + 1],
                                 MISC[0:1, 608 + oc_:609 + oc_]), waits=[])
            else:
                ev(lambda e, oc_=oc:
                   e.tensor_tensor(SCR[0:1, R_QM + oc_:R_QM + oc_ + 1],
                                   MISC[0:1, 608 + oc_:609 + oc_],
                                   SCR[0:1, R_QM + oc_:R_QM + oc_ + 1],
                                   AO.max), waits=[])
            return t1

        emit_proj(wqt, D, D, A, 0, T, [(0, 0, 512), (1, 512, 512)], wr_q,
                  W, 0, ag["wq" if L == 0 else "l1q"], pan_waits=[xn_tok])
        tkn = LS["knorm"]
        if L == 0:
            tcm = ev(lambda e: e.tensor_tensor(SCR[0:1, R_PRD:R_PRD + 16],
                                               SCR[0:1, R_QM:R_QM + 16],
                                               SCR[0:1, R_KMG:R_KMG + 16],
                                               AO.mult), waits=[tkn])
        else:
            tcm = ev(lambda e: e.tensor_tensor(
                AP_(SCR.ap().tensor, R_PRD, [[4096, 1], [4, 4], [1, 4]]),
                AP_(SCR.ap().tensor, R_QM, [[4096, 1], [4, 4], [1, 4]]),
                AP_(SCR.ap().tensor, R_KMG, [[4096, 1], [1, 4], [0, 4]]),
                AO.mult), waits=[tkn])
        tsq2 = ev(lambda e: e.activation(SCR[0:1, R_SQ:R_SQ + 16],
                                         SCR[0:1, R_PRD:R_PRD + 16], AF.Sqrt),
                  waits=[tcm], eng="scalar")
        tcp = ev(lambda e: e.tensor_scalar(SCR[0:1, R_CP:R_CP + 16],
                                           SCR[0:1, R_SQ:R_SQ + 16],
                                           1.05, 2.0, AO.mult, AO.add),
                 waits=[tsq2])
        LS["Cpos"] = tcp
        if L == 0:
            tneg = ev(lambda e: e.tensor_single_scalar(SCR[0:1, R_CN:R_CN + 16],
                                                       SCR[0:1, R_CP:R_CP + 16],
                                                       -1.0, AO.mult), waits=[tcp])
            tbc = mm(ps7[:, 0:16], ONES_ROW, SCR[0:1, R_CN:R_CN + 16],
                     start=True, stop=True, waits=[tneg] + psw(7), inc=True)
            tcb = ev(lambda e: e.tensor_copy(MISC[:, 256:272], ps7[:, 0:16]),
                     waits=[tbc])
            ptok[7] = tcb
            LS["Cneg_b"] = tcb

    def mask_ap(w):
        if w < 12:
            return SCR[:, w * 256:(w + 1) * 256]
        return QH[:, (w - 12) * 256:(w - 11) * 256]

    def build_masks(i):
        t0 = i * QB
        tbq = mm(ps7[:, 0:QB], ONES_ROW, ROWS[0:1, R_INV + t0:R_INV + t0 + QB],
                 start=True, stop=True,
                 waits=[LS["qpos"], t_consts] + psw(7), inc=True)
        tqb = ev(lambda e: e.tensor_copy(MISC[:, 0:QB], ps7[:, 0:QB]), waits=[tbq])
        ptok[7] = tqb
        last = None
        for w in range(16):
            cc, bb = w // 2, w % 2
            kb = cc * 8 + 2 * i + bb
            last = ev(lambda e, o=mask_ap(w),
                      kc=CONS[:, C_KPOS + kb:C_KPOS + kb + 1]:
                      e.tensor_scalar(o, MISC[:, 0:QB], kc, 0.0,
                                      AO.subtract, AO.min), waits=[tqb])
        return last

    attn_ctr = [0]
    pend_tail = [None]

    def attn_tile(L, h, i, tks, tvs, tmask, nkb, prev_out, half_tok):
        ATT = DCH * T
        t0 = i * QB
        kbs = (list(range(64)) if L == 0 else
               [cc * 8 + b for cc in range(8) for b in range(2 * (i + 1))])
        ngr = len(kbs) // 4
        bias_path = (L == 1 and SLOPES[h] * 255.0 < 55.0)
        grpsz = 1 if bias_path else 4
        exp_toks = []
        if L == 1 and not bias_path:
            tbch = ev(lambda e, sl=SLOPES[h]:
                      e.tensor_scalar(MISC[:, 272:273], CONS[:, C_KPOS:C_KPOS + 1],
                                      sl, None, AO.mult),
                      waits=[t_consts] + ([prev_out[0]] if prev_out else []))
        if bias_path:
            tq1 = ev(lambda e, sl=SLOPES[h]:
                     e.tensor_scalar(SCR[0:1, 3468:3469],
                                     ROWS[0:1, R_INV + t0 + QB - 1:R_INV + t0 + QB],
                                     sl, None, AO.mult),
                     waits=[LS["qpos"]])
            tq2 = ev(lambda e, hh=h:
                     e.tensor_scalar(SCR[0:1, 3468:3469], SCR[0:1, 3468:3469],
                                     SCR[0:1, R_CP + hh:R_CP + hh + 1], None,
                                     AO.add), waits=[LS["Cpos"]])
            tqb2 = mm(ps7[:, 0:1], ONES_ROW, SCR[0:1, 3468:3469],
                      start=True, stop=True, waits=[tq2] + psw(7), inc=True)
            tcsb = ev(lambda e: e.tensor_copy(MISC[:, 280:281], ps7[:, 0:1]),
                      waits=[tqb2] + ([prev_out[0]] if prev_out else []))
            ptok[7] = tcsb
            tcsb = ev(lambda e, sl=SLOPES[h]:
                      e.tensor_scalar(MISC[:, 288:352], CONS[:, C_KPOS:C_KPOS + 64],
                                      sl, MISC[:, 280:281], AO.mult, AO.subtract),
                      waits=[t_consts])
        qreg = [(psA, 0, 0), (psA, 512, 1), (psB, 0, 2), (psB, 512, 3)]
        gr_key = [(0, 1), (2, 3)]
        close_prev = [None]

        def emit_group(g):
            slots = [g * 4 + j for j in range(4)]
            if bias_path:
                for j in range(4):
                    n_ = slots[j]
                    kb = kbs[n_]
                    pgq, qo, qkey = qreg[n_ % 4]
                    if n_ % 4 == 0:
                        w0 = list(tks) + psw(0) + psw(1) + psw(2) + psw(3)
                    else:
                        w0 = []
                    tmm = mm(pgq[:, qo:qo + QB], A[:, kb * 128:(kb + 1) * 128],
                             B[:, h * T + t0: h * T + t0 + QB],
                             start=True, stop=True, waits=w0, inc=True)
                    tpre = tmm
                    if (kb % 8) >= 2 * i:
                        w_ = (kb // 8) * 2 + (kb % 8) - 2 * i
                        tpre = ev(lambda e, o=pgq[:, qo:qo + QB], m=mask_ap(w_):
                                  e.scalar_tensor_tensor(o, m, BIGM, o,
                                                         AO.mult, AO.add),
                                  waits=[tmm, tmask])
                    ew = [tpre, tcsb]
                    if n_ == 0:
                        ew += [half_tok] + ([prev_out[1]] if prev_out else [])
                    te = ev(lambda e, o=B[:, ATT + n_ * QB:ATT + (n_ + 1) * QB],
                            i_=pgq[:, qo:qo + QB], b=MISC[:, 288 + kb:289 + kb]:
                            e.activation(o, i_, AF.Exp, bias=b),
                            waits=ew, eng="scalar")
                    ptok[qkey] = te
                    exp_toks.append(te)
                return
            pg = [psA, psB][g % 2]
            key = gr_key[g % 2]
            w0 = list(tks) + psw(key[0]) + psw(key[1])
            if L == 1:
                tcr = None
                for j in range(4):
                    kb = kbs[slots[j]]
                    offc = float(-SLOPES[h] * 128.0 * KB_GBLK[kb])
                    ev(lambda e, j_=j, sl=SLOPES[h], of=offc:
                       e.tensor_scalar(ROWS[0:1, j_ * 256:(j_ + 1) * 256],
                                       ROWS[0:1, R_INV + t0:R_INV + t0 + QB],
                                       sl, of, AO.mult, AO.add),
                       waits=[LS["qpos"]] +
                       ([close_prev[0]] if (j == 0 and close_prev[0]) else []))
                    tcr = ev(lambda e, j_=j, hh=h:
                             e.tensor_scalar(ROWS[0:1, j_ * 256:(j_ + 1) * 256],
                                             ROWS[0:1, j_ * 256:(j_ + 1) * 256],
                                             SCR[0:1, R_CP + hh:R_CP + hh + 1],
                                             None, AO.add),
                             waits=[LS["Cpos"]])
                tclose = None
                for hf in range(2):
                    mm(pg[:, hf * 512:(hf + 1) * 512], NEG_ROW,
                       ROWS[0:1, hf * 512:(hf + 1) * 512],
                       start=True, stop=False,
                       waits=(w0 if hf == 0 else []) + [tcr])
                    for j in (hf * 2, hf * 2 + 1):
                        kb = kbs[slots[j]]
                        tclose = mm(pg[:, j * QB:(j + 1) * QB],
                                    A[:, kb * 128:(kb + 1) * 128],
                                    B[:, h * T + t0: h * T + t0 + QB],
                                    start=False, stop=(j == hf * 2 + 1),
                                    waits=[],
                                    inc=True if j == hf * 2 + 1 else None)
            else:
                tclose = None
                for j in range(4):
                    kb = kbs[slots[j]]
                    tclose = mm(pg[:, j * QB:(j + 1) * QB],
                                A[:, kb * 128:(kb + 1) * 128],
                                B[:, h * T + t0: h * T + t0 + QB],
                                start=True, stop=True,
                                waits=(w0 if j == 0 else []),
                                inc=True if j == 3 else None)
            close_prev[0] = tclose
            tpre = tclose
            if L == 1:
                for j in range(4):
                    kb = kbs[slots[j]]
                    if (kb % 8) >= 2 * i:
                        w_ = (kb // 8) * 2 + (kb % 8) - 2 * i
                        tpre = ev(lambda e, o=pg[:, j * QB:(j + 1) * QB],
                                  m=mask_ap(w_):
                                  e.scalar_tensor_tensor(o, m, BIGM, o,
                                                         AO.mult, AO.add),
                                  waits=[tclose, tmask])
            bias_ap = (MISC[:, 256 + h:257 + h] if L == 0 else MISC[:, 272:273])
            bw = [tpre] + ([LS["Cneg_b"]] if L == 0 else [tbch])
            if g == 0:
                bw += [half_tok] + ([prev_out[1]] if prev_out else [])
            te = ev(lambda e, o=B[:, ATT + g * 1024:ATT + (g + 1) * 1024],
                    i_=pg[:, 0:1024], b=bias_ap:
                    e.activation(o, i_, AF.Exp, bias=b), waits=bw, eng="scalar")
            ptok[key[0]] = te
            ptok[key[1]] = te
            exp_toks.append(te)

        red_state = {"den": None, "av": None}

        def emit_red(g):
            first = (g == 0)
            last = (g == ngr - 1)
            for pp in range(2):
                p2 = g * 2 + pp
                wde = [exp_toks[(2 * p2 + 1) // grpsz]]
                if p2 == 0:
                    wde += psw(5)
                red_state["den"] = mm(psD[0:1, 0:512], ONE_COL_B,
                                      B[:, ATT + p2 * 512:ATT + (p2 + 1) * 512],
                                      start=(p2 == 0), stop=(last and pp == 1),
                                      waits=wde,
                                      inc=True if (last and pp == 1) else None)
            for j in range(4):
                n_ = g * 4 + j
                kb = kbs[n_]
                wav = []
                if n_ == 0:
                    wav = list(tvs) + psw(4)
                red_state["av"] = mm(psC[:, 0:QB],
                                     A[:, 8192 + kb * 128: 8192 + (kb + 1) * 128],
                                     B[:, ATT + n_ * QB:ATT + (n_ + 1) * QB],
                                     start=(n_ == 0), stop=(last and j == 3),
                                     waits=wav,
                                     inc=True if (last and j == 3) else None)

        for g in range(ngr):
            emit_group(g)
            if g == 0 and pend_tail[0] is not None:
                pend_tail[0]()
                pend_tail[0] = None
            if g >= 2:
                emit_red(g - 2)
        emit_red(ngr - 2)
        emit_red(ngr - 1)
        tden = red_state["den"]
        tav = red_state["av"]
        par = attn_ctr[0] % 2
        attn_ctr[0] += 1
        frow = 3584 + par * 256
        ev(lambda e, fr=frow: e.tensor_copy(SCR[0:1, fr:fr + 256],
                                            psD[0:1, 0:256]), waits=[tden])
        tfd = ev(lambda e, fr=frow: e.tensor_tensor(SCR[0:1, fr:fr + 256],
                                                    SCR[0:1, fr:fr + 256],
                                                    psD[0:1, 256:512], AO.add),
                 waits=[])
        ptok[5] = tfd

        def tail(_tfd=tfd, _tav=tav, _h=h, _i=i, _ht=half_tok, _fr=frow):
            trc = ev(lambda e: e.reciprocal(SCR[0:1, R_RCP:R_RCP + QB],
                                            SCR[0:1, _fr:_fr + 256]),
                     waits=[_tfd])
            tbc2 = mm(ps7[:, 0:QB], ONES_ROW, SCR[0:1, R_RCP:R_RCP + QB],
                      start=True, stop=True, waits=[trc] + psw(7), inc=True)
            trb = ev(lambda e: e.copy(MISC[:, 352:352 + QB], ps7[:, 0:QB]),
                     waits=[tbc2], eng="scalar")
            ptok[7] = trb
            tno = ev(lambda e, o=W[:, _h * 512 + (_i % 2) * QB:
                                   _h * 512 + (_i % 2) * QB + QB],
                     a=psC[:, 0:QB], b=MISC[:, 352:352 + QB]:
                     e.tensor_tensor(o, a, b, AO.mult),
                     waits=[_tav, trb, _ht])
            ptok[4] = tno
            return tno

        pend_tail[0] = tail
        return (exp_toks[-1], tav, tail)

    def oproj_half(L, half, last_norm):
        wot = dint["wo_t"] if L == 0 else dint["l1o_t"]
        bias_o = (C_BIAS + 48) if L == 0 else None
        t0 = half * 512
        res = [None]

        def wr_o(oc, tb, tt0, tw, psa, tmm):
            dst = X[:, oc * T + t0: oc * T + t0 + 512]
            if bias_o is not None:
                t1 = ev(lambda e, o=dst, i_=psa,
                        bc=CONS[:, bias_o + oc:bias_o + oc + 1]:
                        e.scalar_tensor_tensor(o, i_, bc, o, AO.add, AO.add),
                        waits=[tmm])
            else:
                t1 = ev(lambda e, o=dst, i_=psa:
                        e.tensor_tensor(o, i_, o, AO.add), waits=[tmm])
            res[0] = t1
            return t1

        emit_proj(wot, D, D, W, 0, 512, [(0, 0, 512)], wr_o,
                  B, DCH * T, ag["wo" if L == 0 else "l1o"],
                  pan_waits=[last_norm])
        return res[0]

    def phase_attn(L):
        kag = dint["k0_ag"] if L == 0 else dint["k1_ag"]
        vag = dint["v0_ag"] if L == 0 else dint["v1_ag"]
        kd = D if L == 0 else 512
        nvh = NH if L == 0 else KVH
        tk_ag, tv_ag = LS["kv_ag"]
        slab = {"kh": -1, "vh": -1, "k": None, "v": None,
                "kcons": None, "vcons": None}

        def load_k(h):
            kh = h if L == 0 else h // 4
            if slab["kh"] == kh:
                return slab["k"]
            tkk = [None, None]
            for cc in range(8):
                tkk[cc % 2] = p.dma("scalar" if cc % 2 == 0 else "gpsimd",
                                    A[:, cc * 1024:(cc + 1) * 1024],
                                    kag[cc * kd + kh * 128: cc * kd + (kh + 1) * 128, :],
                                    waits=[tk_ag, slab["kcons"]] if cc < 2 else [],
                                    new_group=(cc < 2))
            slab["kh"] = kh
            slab["k"] = tkk
            return tkk

        def load_v(h):
            vh = h if L == 0 else h // 4
            if slab["vh"] == vh:
                return slab["v"]
            tvv = [None, None]
            for cc in range(8):
                tvv[cc % 2] = p.dma("sync" if cc % 2 == 0 else "gpsimd",
                                    A[:, 8192 + cc * 1024: 8192 + (cc + 1) * 1024],
                                    AP_(vag, (cc * nvh + vh) * T * DH,
                                        [[DH, 128], [128 * DH, 8], [1, DH]]),
                                    waits=[tv_ag, slab["vcons"]] if cc < 2 else [],
                                    new_group=(cc < 2))
            slab["vh"] = vh
            slab["v"] = tvv
            return tvv

        half_tok = None

        def run_tile(h, i, tks, tvs, tmask, prev, ht):
            return attn_tile(L, h, i, tks, tvs, tmask, 0, prev, ht)

        for half in range(2):
            outs = []
            if L == 0:
                for h in range(NH):
                    tks = load_k(h)
                    tvs = load_v(h)
                    for i in ((0, 1) if half == 0 else (2, 3)):
                        o = run_tile(h, i, tks, tvs, None,
                                     outs[-1] if outs else None, half_tok)
                        outs.append(o)
                    slab["kcons"] = o[0]
                    slab["vcons"] = o[1]
                    slab["kh"] = -1 if h == NH - 1 else slab["kh"]
                    slab["vh"] = -1 if h == NH - 1 else slab["vh"]
            else:
                for i in ((0, 1) if half == 0 else (2, 3)):
                    tmask = build_masks(i)
                    for h in range(NH):
                        tks = load_k(h)
                        tvs = load_v(h)
                        o = run_tile(h, i, tks, tvs, tmask,
                                     outs[-1] if outs else None, half_tok)
                        outs.append(o)
                        slab["kcons"] = o[0]
                        slab["vcons"] = o[1]
                    slab["kh"] = -1
                    slab["vh"] = -1
            last_tno = pend_tail[0]()
            pend_tail[0] = None
            half_tok = oproj_half(L, half, last_tno)
        p.barrier()

    prep_one("wk")
    prep_one("wv")
    phase_ln(0)
    p.barrier()
    phase_bvb()
    phase_kv(0)
    prep_rest()
    p.barrier()
    phase_ffn(0)
    phase_q(0)
    p.barrier()
    phase_attn(0)
    phase_ln(1)
    p.barrier()
    phase_kv(1)
    p.barrier()
    phase_ffn(1)
    phase_q(1)
    p.barrier()
    phase_attn(1)

    scr_cons2 = None
    for tch in range(8):
        for grp in range(4):
            bank = grp % 2
            tmm = None
            for j in range(4):
                dcb = grp * 4 + j
                tmm = p.emit(
                    "tensor",
                    lambda e, o=ps[bank][:, j * 128:(j + 1) * 128],
                    i_=X[:, dcb * T + tch * 128: dcb * T + (tch + 1) * 128],
                    idn=CONS[:, C_ID:C_ID + 128]: e.transpose(o, i_, idn),
                    waits=psw(bank) if j == 0 else [],
                    inc=True if j == 3 else None)
            tcp = ev(lambda e, o=SCR[:, grp * 512:(grp + 1) * 512],
                     i_=ps[bank][:, 0:512]: e.tensor_copy(o, i_),
                     waits=[tmm, scr_cons2])
            ptok[bank] = tcp
        tst = p.dma("sync", y[tch * 128:(tch + 1) * 128, :], SCR[:, 0:2048],
                    waits=[ptok[0], ptok[1]])
        scr_cons2 = tst
    for q, eng in (("sync", "sync"), ("gpsimd", "gpsimd"), ("scalar", "scalar")):
        p.wait_only(eng, [(sm, sm.n) for sm in p.dq[q]["ring"] if sm.n > 0])

    blk = nc.Block()
    blk.__enter__()
    for eng in ENGS:
        getattr(blk, eng)(lambda e, _ops=tuple(p.ops[eng]): [f(e) for f in _ops])
    blk.__exit__(None, None, None)
    for g in reversed(ctxs):
        g.__exit__(None, None, None)
    return nc


def kernel(**inputs):
    I = {k: np.asarray(v) for k, v in inputs.items()}
    Xall = np.ascontiguousarray(I["L0_attn_X"].reshape(S, D))
    wdp = np.ascontiguousarray(I["L1_ffn_w_down"][PERM, :])
    wgp = np.ascontiguousarray(I["L1_ffn_w_gate"][:, PERM])
    wup = np.ascontiguousarray(I["L1_ffn_w_up"][:, PERM])
    in_maps = []
    for c in range(NCORES):
        idx = tok_idx(c)
        s0, s1 = c * 256, (c + 1) * 256
        m = {
            "x": np.ascontiguousarray(Xall[idx]),
            "qpos": np.ascontiguousarray(idx.astype(np.float32)),
            "wq_sh": np.ascontiguousarray(I["L0_attn_Wq"][:, s0:s1]),
            "wk_sh": np.ascontiguousarray(I["L0_attn_Wk"][:, s0:s1]),
            "wv_sh": np.ascontiguousarray(I["L0_attn_Wv"][:, s0:s1]),
            "wo_sh": np.ascontiguousarray(I["L0_attn_Wo"][:, s0:s1]),
            "w1_sh": np.ascontiguousarray(I["L0_ffn_W1"][:, s0:s1]),
            "w3_sh": np.ascontiguousarray(I["L0_ffn_W3"][:, s0:s1]),
            "w2_sh": np.ascontiguousarray(I["L0_ffn_W2"][:, c * 1024:(c + 1) * 1024]),
            "l1q_sh": np.ascontiguousarray(I["L1_attn_q_w"][:, s0:s1]),
            "l1k_sh": np.ascontiguousarray(I["L1_attn_k_w"][:, s0:s1]),
            "l1v_sh": np.ascontiguousarray(I["L1_attn_v_w"][:, s0:s1]),
            "l1o_sh": np.ascontiguousarray(I["L1_attn_o_w"][:, s0:s1]),
            "wg_sh": np.ascontiguousarray(wgp[s0:s1, :]),
            "wu_sh": np.ascontiguousarray(wup[s0:s1, :]),
            "wd_sh": np.ascontiguousarray(wdp[c * 1024:(c + 1) * 1024, :]),
            "bq": np.ascontiguousarray(I["L0_attn_bq"]),
            "bk": np.ascontiguousarray(I["L0_attn_bk"]),
            "bv": np.ascontiguousarray(I["L0_attn_bv"]),
            "bo": np.ascontiguousarray(I["L0_attn_bo"]),
        }
        in_maps.append(m)
    nc = build_nc()
    res = run_bass_kernel_spmd(nc, in_maps, core_ids=list(range(NCORES)))
    out = np.empty((S, D), np.float32)
    for c in range(NCORES):
        out[tok_idx(c)] = res.results[c]["y"]
    return out.reshape(1, S, D)
